# version 25
# baseline (speedup 1.0000x reference)
import os
os.environ.setdefault("TRN_ALLOW_ICI_ALL_TO_ALL", "1")

import contextlib
import numpy as np
import ml_dtypes

import concourse.bass as bass
import concourse.mybir as mybir
from concourse.bass_utils import run_bass_kernel_spmd

F32 = mybir.dt.float32
BF16 = mybir.dt.bfloat16
AF = mybir.ActivationFunctionType
ALU = mybir.AluOpType
BF = ml_dtypes.bfloat16

D = 1024
S = 2048
B = 8
H = 16
E = 64
NT = 16
NCH = 8
NPK = 8
R = 256
EPS1 = 1e-5
EPS2 = 1e-3
EXPB = -20.0


class Sem:
    def __init__(self, nc, es, name):
        self.h = es.enter_context(nc.semaphore(name))
        self.n = 0

    def inc(self, bi, k=1):
        bi.then_inc(self.h, k)
        self.n += k

    def wait(self, eng, n=None):
        v = self.n if n is None else n
        if v > 0:
            eng.wait_ge(self.h, v)


def build_nc(dbg=False):
    nc = bass.Bass("TRN2", target_bir_lowering=False)

    dt = lambda n, sh, k="ExternalInput", d=F32: nc.dram_tensor(n, sh, d, kind=k)
    x0_d = dt("x0", [S, D])
    c9_d = dt("c9", [9, D])
    wmod_d = dt("wmod", [D, 6 * D])
    wq_d = dt("wq", [D, D]); wk_d = dt("wk", [D, D]); wv_d = dt("wv", [D, D])
    wo_d = dt("wo", [D, D])
    gam_d = dt("gamma", [1, D]); bet_d = dt("beta", [1, D])
    wfT_d = dt("wfT", [D, D]); ufT_d = dt("ufT", [D, D])
    wiT_d = dt("wiT", [D, D]); uiT_d = dt("uiT", [D, D])
    wcT_d = dt("wcT", [D, D]); ucT_d = dt("ucT", [D, D])
    woT_d = dt("woT", [D, D]); uoT_d = dt("uoT", [D, D])
    brow_d = dt("brows", [5, D])
    ident_d = dt("ident", [128, 128], d=BF16)
    mask_d = dt("masktile", [128, 128], d=BF16)
    ones_d = dt("onesrow", [1, 512], d=BF16)
    out_d = dt("out", [B, R, D], k="ExternalOutput")
    x1send_h = [nc.dram_tensor("x1sendA", [B, 128, D], F32),
                nc.dram_tensor("x1sendB", [B, 128, D], F32)]
    h1tok_d = nc.dram_tensor("h1tok", [S, D], BF16)
    x1recv_h = [nc.dram_tensor("x1recvA", [B, 128, D], F32),
                nc.dram_tensor("x1recvB", [B, 128, D], F32)]
    if dbg:
        dbg_mod = nc.dram_tensor("dbg_mod", [9, 6 * D], F32, kind="ExternalOutput")
        dbg_x1 = nc.dram_tensor("dbg_x1", [S, D], F32, kind="ExternalOutput")
        dbg_c = nc.dram_tensor("dbg_c", [128, 2048], F32, kind="ExternalOutput")
        dbg_h2 = nc.dram_tensor("dbg_h2", [128, B * 2048], F32, kind="ExternalOutput")

    es = contextlib.ExitStack()
    with es:
        sb = lambda n, sh, d=BF16: es.enter_context(nc.sbuf_tensor(n, sh, d))
        ident = sb("identsb", [128, 128])
        mask = sb("masksb", [128, 128])
        onesr = sb("onesrsb", [1, 512])
        km20 = sb("km20", [128, 1], F32)
        kmhalf = sb("kmhalf", [128, 1], F32)
        mod = sb("modsb", [9, 6 * D])
        mp0a = sb("mp0a", [1, D])
        mp0b = sb("mp0b", [1, D])
        st1 = sb("st1", [128, 12], F32); ag1 = sb("ag1", [128, 2], F32)
        ve1 = sb("ve1", [128, 1], F32); vi1 = sb("vi1", [128, 1], F32)
        st2 = sb("st2", [128, 12], F32); ag2 = sb("ag2", [128, 2], F32)
        ve2 = sb("ve2", [128, 1], F32); vi2 = sb("vi2", [128, 1], F32)

        sm = lambda n: Sem(nc, es, n)
        dcon = sm("dcon"); dxr = [sm("dxr0"), sm("dxr1")]; dxf = sm("dxf"); dgwm = sm("dgwm")
        dx0s = [sm(f"dx0s{i}") for i in range(3)]
        dx1s = [sm(f"dx1s{i}") for i in range(2)]
        dout = [sm("dout0"), sm("dout1")]; dscq = sm("dscq"); dgw = sm("dgw"); dgf = sm("dgf")
        ccs = sm("ccs"); ccs2 = sm("ccs2")
        kons = sm("kons")

        T = nc.tensor; V = nc.vector; C = nc.scalar; G = nc.gpsimd; Y = nc.sync

        blk = es.enter_context(nc.Block())

        @blk.sync
        def _(sync):
            V.memset(km20[:], EXPB)
            V.memset(kmhalf[:], -0.5)
            kons.inc(V.drain())
            kons.wait(C)
            kons.wait(G)
            bi = Y.dma_start(out=ident[:], in_=ident_d.ap()[:, :]); dcon.inc(bi, 16)
            bi = Y.dma_start(out=mask[:], in_=mask_d.ap()[:, :]); dcon.inc(bi, 16)
            bi = Y.dma_start(out=onesr[:], in_=ones_d.ap()[:, :]); dcon.inc(bi, 16)

            svA = sm("svA"); spA = sm("spA"); spB = sm("spB")
            ssA = sm("ssA"); ssB = sm("ssB"); ssC = sm("ssC")

            with contextlib.ExitStack() as st_es:
                sbl = lambda n, sh, d=BF16: st_es.enter_context(nc.sbuf_tensor(n, sh, d))
                c9f = sbl("c9f", [9, D], F32)
                siluc = sbl("siluc", [9, D])
                silucT = sbl("silucT", [128, 9 * NCH])
                ghelp = sbl("ghelp", [9, 2 * D])
                cblk = sbl("cblk", [128, 8 * 6144])
                xps = st_es.enter_context(nc.psum_tensor("xps", [128, 128], BF16))
                mps = [st_es.enter_context(nc.psum_tensor(f"mps{i}", [9, 512], F32)) for i in range(2)]

                bi = Y.dma_start(out=c9f[:], in_=c9_d.ap()[:, :]); dcon.inc(bi, 16)
                wmv = wmod_d.ap().rearrange("(c p) n -> p c n", p=128)
                dgwm4 = [sm(f"dgwm4{i}") for i in range(4)]
                for s4 in range(4):
                    bi = G.dma_start(
                        out=cblk[:].rearrange("p (c n) -> p c n", c=8)[:, :, 1536 * s4:1536 * (s4 + 1)],
                        in_=wmv[:, :, 1536 * s4:1536 * (s4 + 1)])
                    dgwm4[s4].inc(bi, 16)

                dcon.wait(C)
                C.activation(out=siluc[:], in_=c9f[:], func=AF.Silu)
                ssA.inc(C.drain())
                ssA.wait(T)
                for c in range(NCH):
                    T.transpose(xps[0:128, 16 * c:16 * c + 9], siluc[0:9, 128 * c:128 * (c + 1)], ident[0:9, 0:9])
                spA.inc(T.drain())
                spA.wait(C)
                C.copy(out=silucT[:].rearrange("p (c n) -> p c n", c=NCH),
                       in_=xps[:].rearrange("p (c n) -> p c n", c=NCH)[:, :, 0:9])
                ssB.inc(C.drain())
                ssB.wait(T)
                for j in range(12):
                    T.wait_ge(dgwm4[j // 3].h, 16)
                    if j >= 2:
                        T.wait_ge(ssC.h, j - 1)
                    mp = mps[j % 2]
                    for c in range(NCH):
                        T.matmul(mp[0:9, 0:512], silucT[:, 9 * c:9 * (c + 1)],
                                 cblk[:, 6144 * c + 512 * j:6144 * c + 512 * (j + 1)],
                                 start=(c == 0), stop=(c == NCH - 1))
                    spB.inc(T.drain())
                    spB.wait(C)
                    C.copy(out=mod[0:9, 512 * j:512 * (j + 1)], in_=mp[0:9, 0:512])
                    ssC.inc(C.drain())
                ssC.wait(V)
                for lo in (D, 4 * D):
                    V.tensor_scalar(out=mod[0:9, lo:lo + D], in0=mod[0:9, lo:lo + D],
                                    scalar1=1.0, scalar2=None, op0=ALU.add)
                    V.drain()
                for lo in (2 * D, 5 * D):
                    sl = mod[0:9, lo:lo + D]
                    V.tensor_scalar(out=ghelp[0:9, 0:D], in0=sl, scalar1=0.0, scalar2=2.0,
                                    op0=ALU.is_ge, op1=ALU.mult)
                    V.drain()
                    V.tensor_scalar(out=ghelp[0:9, 0:D], in0=ghelp[0:9, 0:D], scalar1=-1.0,
                                    scalar2=None, op0=ALU.add)
                    V.drain()
                    V.tensor_scalar(out=ghelp[0:9, D:2 * D], in0=sl, scalar1=-1.0, scalar2=None,
                                    op0=ALU.mult)
                    V.drain()
                    V.tensor_tensor(out=ghelp[0:9, D:2 * D], in0=sl, in1=ghelp[0:9, D:2 * D], op=ALU.max)
                    V.drain()
                    V.tensor_scalar(out=ghelp[0:9, D:2 * D], in0=ghelp[0:9, D:2 * D], scalar1=0.25,
                                    scalar2=None, op0=ALU.max)
                    V.drain()
                    V.tensor_tensor(out=sl, in0=ghelp[0:9, 0:D], in1=ghelp[0:9, D:2 * D], op=ALU.mult)
                    V.drain()
                svA.inc(V.drain())
                if dbg:
                    svA.wait(G)
                    bi = G.dma_start(out=dbg_mod.ap()[:, :], in_=mod[0:9, :])
                    dgw.inc(bi, 16)

            pE = sm("pE"); x1dn = sm("x1dn"); hTd = sm("hTd")
            vh1 = sm("vh1"); xprdy = sm("xprdy"); h1wld = sm("h1wld")
            pjev = sm("pjev"); pjdn = sm("pjdn"); p1dn = sm("p1dn")
            dh1w = [sm("dh1w0"), sm("dh1w1")]; dh1r = [sm("dh1r0"), sm("dh1r1")]
            y1d = sm("y1d"); g1d = sm("g1d"); yLd = sm("yLd"); gLd = sm("gLd")
            vLN = sm("vLN")
            srdy = sm("srdy"); expd = sm("expd"); mskd = sm("mskd")
            avd = sm("avd"); nrm = sm("nrm"); oxp = sm("oxp")
            wod = sm("wod"); vD = sm("vD")
            pC = sm("pC"); sD = sm("sD"); vB = sm("vB")
            dx0r = [sm(f"dx0r{i}") for i in range(2)]

            with contextlib.ExitStack() as ao_es:
                sbo = lambda n, sh, d=BF16: ao_es.enter_context(nc.sbuf_tensor(n, sh, d))
                kt = sbo("ktsb", [128, NPK * S])
                vex = sbo("vexsb", [128, NT * (H * 65)])
                qtF = sbo("qtFsb", [128, NPK * S])
                wosb = ao_es.enter_context(nc.sbuf_tensor("wosb", [128, 8192], BF16, side="right"))

                with contextlib.ExitStack() as p1_es:
                    sb1 = lambda n, sh, d=BF16: p1_es.enter_context(nc.sbuf_tensor(n, sh, d))
                    sb1r = lambda n, sh, d=BF16: p1_es.enter_context(nc.sbuf_tensor(n, sh, d, side="right"))
                    wq = sb1r("wqsb", [128, 8192]); wk = sb1r("wksb", [128, 8192]); wv = sb1r("wvsb", [128, 8192])
                    h1wW = sb1("h1wW", [128, 8 * 512])
                    x0s = [sb1(f"x0s{i}", [128, D], F32) for i in range(3)]
                    hb1 = [sb1(f"hb1{i}", [128, D]) for i in range(2)]
                    ca1 = sb1("ca1sb", [128, D]); sav = sb1("savsb", [128, D])

                    for (dst, srcd) in ((wq, wq_d), (wk, wk_d), (wv, wv_d), (wosb, wo_d)):
                        bi = G.dma_start(out=dst[:].rearrange("p (c n) -> p c n", c=8),
                                         in_=srcd.ap().rearrange("(c p) n -> p c n", p=128))
                        dgw.inc(bi, 16)
                    V.wait_ge(spB.h, 12)
                    V.memset(vex[:].rearrange("p (s c) -> p s c", c=65)[:, :, 64:65], 1.0)
                    vB.inc(V.drain())

                    xpA = p1_es.enter_context(nc.psum_tensor("xpA", [128, 1024], BF16))
                    xpB = p1_es.enter_context(nc.psum_tensor("xpB", [128, 1024], BF16))
                    pj1 = p1_es.enter_context(nc.psum_tensor("pj1", [128, 512], F32))
                    pj2 = p1_es.enter_context(nc.psum_tensor("pj2", [128, 512], F32))
                    pjV = p1_es.enter_context(nc.psum_tensor("pjV", [128, 1024], F32))

                    dgw.wait(T)
                    dcon.wait(T)
                    svA.wait(C)
                    for dstt, lo in ((sav, 0), (ca1, D)):
                        C.wait_ge(pC.h, pC.n)
                        bi = C.dma_start(out=mp0a[:], in_=mod[8:9, lo:lo + D]); dscq.inc(bi, 16)
                        dscq.wait(T)
                        for hh in range(2):
                            sD.wait(T)
                            T.matmul(pj1[:, 0:512], onesr[0:1, 0:128],
                                     mp0a[0:1, 512 * hh:512 * (hh + 1)], start=True, stop=True)
                            pC.inc(T.drain())
                            pC.wait(C)
                            C.copy(out=dstt[:, 512 * hh:512 * (hh + 1)], in_=pj1[:, 0:512])
                            sD.inc(C.drain())

                    Y.wait_ge(spB.h, 12)
                    for k in range(2):
                        bi = Y.dma_start(out=x0s[k][:], in_=x0_d.ap()[128 * k:128 * (k + 1), :])
                        dx0s[k].inc(bi, 16)

                    sD.wait(V)
                    pj_hist = {}
                    for w in range(4):
                        for k in range(4):
                            ti = 4 * w + k
                            if ti + 2 < NT:
                                if ti >= 1:
                                    Y.wait_ge(vLN.h, ti)
                                bi = Y.dma_start(out=x0s[(ti + 2) % 3][:],
                                                 in_=x0_d.ap()[128 * (ti + 2):128 * (ti + 3), :])
                                dx0s[(ti + 2) % 3].inc(bi, 16)
                            dx0s[ti % 3].wait(V)
                            xs = x0s[ti % 3]
                            V.bn_stats(st1[:, 0:6], xs[:, 0:512]); V.drain()
                            V.bn_stats(st1[:, 6:12], xs[:, 512:1024]); V.drain()
                            V.bn_aggr(ag1[:], st1[:]); V.drain()
                            V.tensor_scalar(out=ve1[:], in0=ag1[:, 1:2], scalar1=EPS1, scalar2=None, op0=ALU.add)
                            y1d.inc(V.drain())
                            y1d.wait(G)
                            G.tensor_tensor(out=vi1[:], in0=ve1[:], in1=kmhalf[:], op=ALU.pow)
                            g1d.inc(G.drain())
                            g1d.wait(V)
                            if ti >= 2:
                                V.wait_ge(xprdy.h, ti - 1)
                                V.wait_ge(dh1w[ti % 2].h, 16 * (ti // 2))
                            V.tensor_scalar(out=hb1[k % 2][:], in0=xs[:], scalar1=ag1[:, 0:1],
                                            scalar2=vi1[:], op0=ALU.subtract, op1=ALU.mult)
                            V.drain()
                            vLN.inc(V.drain())
                            V.tensor_tensor(out=hb1[k % 2][:], in0=hb1[k % 2][:], in1=ca1[:], op=ALU.mult); V.drain()
                            V.tensor_tensor(out=hb1[k % 2][:], in0=hb1[k % 2][:], in1=sav[:], op=ALU.add)
                            vh1.inc(V.drain())
                            vh1.wait(Y)
                            bi = Y.dma_start(out=h1tok_d.ap()[128 * ti:128 * (ti + 1), :], in_=hb1[k % 2][:])
                            dh1w[ti % 2].inc(bi, 16)
                            vh1.wait(T)
                            xp = xpA if k % 2 == 0 else xpB
                            if ti >= 2:
                                T.wait_ge(h1wld.h, ti - 1)
                            for c in range(NCH):
                                T.transpose(xp[:, 128 * c:128 * (c + 1)], hb1[k % 2][:, 128 * c:128 * (c + 1)], ident[:])
                            xprdy.inc(T.drain())
                            xprdy.wait(C)
                            C.copy(out=h1wW[:].rearrange("p (c n) -> p c n", c=NCH)[:, :, 128 * k:128 * (k + 1)],
                                   in_=xp[:].rearrange("p (c n) -> p c n", c=NCH))
                            h1wld.inc(C.drain())

                        h1wld.wait(T)
                        for kind, wsrc, dst in (("K", wk, kt), ("Q", wq, qtF)):
                            for p in range(NPK):
                                pj = pj1 if p % 2 == 0 else pj2
                                prev = pj_hist.get(("KQ", p % 2))
                                if prev:
                                    T.wait_ge(pjev.h, prev)
                                for c in range(NCH):
                                    T.matmul(pj[:, 0:512],
                                             wsrc[:, 1024 * c + 128 * p:1024 * c + 128 * (p + 1)],
                                             h1wW[:, 512 * c:512 * (c + 1)],
                                             start=(c == 0), stop=(c == NCH - 1),
                                             skip_group_check=True)
                                pjdn.inc(T.drain())
                                pjdn.wait(C)
                                C.copy(out=dst[:, 2048 * p + 512 * w:2048 * p + 512 * (w + 1)], in_=pj[:, 0:512])
                                pjev.inc(C.drain())
                                pj_hist[("KQ", p % 2)] = pjev.n
                        for k in range(4):
                            ti = 4 * w + k
                            prev = pj_hist.get("V")
                            if prev:
                                T.wait_ge(pjev.h, prev)
                            for c in range(NCH):
                                for half in range(2):
                                    T.matmul(pjV[:, 512 * half:512 * (half + 1)],
                                             h1wW[:, 512 * c + 128 * k:512 * c + 128 * (k + 1)],
                                             wv[:, 1024 * c + 512 * half:1024 * c + 512 * (half + 1)],
                                             start=(c == 0), stop=(c == NCH - 1),
                                             skip_group_check=True)
                            pjdn.inc(T.drain())
                            pjdn.wait(C)
                            C.copy(out=vex[:].rearrange("p (s c) -> p s c", c=65)[:, H * ti:H * (ti + 1), 0:64],
                                   in_=pjV[:].rearrange("p (s c) -> p s c", c=64))
                            pjev.inc(C.drain())
                            pj_hist["V"] = pjev.n
                    p1dn.inc(T.drain())

                with contextlib.ExitStack() as p2_es:
                    sb2 = lambda n, sh, d=BF16: p2_es.enter_context(nc.sbuf_tensor(n, sh, d))
                    NATR = 16
                    atr = [sb2(f"atr{i}", [128, 1024]) for i in range(NATR)]
                    hTb = sb2("hTb", [128, 4 * D])
                    osb1 = sb2("osb1", [128, 512])
                    linv = [sb2(f"linvsb{i}", [128, 4], F32) for i in range(2)]
                    yt = sb2("ytsb", [128, D], F32)
                    z2 = sb2("z2sb", [128, D])
                    ab = sb2("absb", [128, D])
                    x1t = [sb2(f"x1t{i}", [128, D], F32) for i in range(2)]
                    x0r = [sb2(f"x0r{i}", [128, D], F32) for i in range(2)]
                    h1r = [sb2(f"h1r{i}", [128, D]) for i in range(2)]
                    gav = sb2("gavsb", [128, D]); gmt = sb2("gmtsb", [128, D]); btt = sb2("bttsb", [128, D])
                    qz = [sb2(f"qz{i}", [128, 512]) for i in range(2)]

                    sC = [p2_es.enter_context(nc.psum_tensor(f"sC{i}", [128, 1024], F32)) for i in range(3)]
                    oPdb = p2_es.enter_context(nc.psum_tensor("oPdb", [128, 512], F32))
                    oPd = [oPdb, oPdb]
                    woP = p2_es.enter_context(nc.psum_tensor("woP", [128, 512], F32))

                    bi = G.dma_start(out=mp0a[:], in_=gam_d.ap()[:, :]); dgw.inc(bi, 16)
                    bi = G.dma_start(out=mp0b[:], in_=bet_d.ap()[:, :]); dgw.inc(bi, 16)
                    dgw.wait(T)
                    if True:
                        bp2 = woP
                        def bc2(dstt, srcrow):
                            for hh in range(2):
                                sD.wait(T)
                                T.matmul(bp2[:, 0:512], onesr[0:1, 0:128],
                                         srcrow[0:1, 512 * hh:512 * (hh + 1)], start=True, stop=True)
                                pC.inc(T.drain())
                                pC.wait(C)
                                C.copy(out=dstt[:, 512 * hh:512 * (hh + 1)], in_=bp2[:, 0:512])
                                sD.inc(C.drain())

                        bc2(gmt, mp0a)
                        bc2(btt, mp0b)
                        C.wait_ge(pC.h, pC.n)
                        bi = C.dma_start(out=mp0a[:], in_=mod[8:9, 2 * D:3 * D]); dscq.inc(bi, 16)
                        dscq.wait(T)
                        bc2(gav, mp0a)
                        sD.wait(V)

                    C.wait_ge(dh1w[0].h, 16 * NT // 2)
                    C.wait_ge(dh1w[1].h, 16 * NT // 2)
                    sc_hist = {}
                    atr_hist = {}
                    nrm_hist = {}
                    hT_hist = {}
                    qz_hist = {}
                    oxp_hist = {}
                    yadd = sm("yadd")
                    qzd = sm("qzd")
                    p1dn.wait(V)
                    qzm = sm("qzm")
                    V.memset(qz[0][:], 0.0); V.memset(qz[1][:], 0.0)
                    qzm.inc(V.drain())
                    qzm.wait(Y)
                    qzdd = [sm("qzdd0"), sm("qzdd1")]

                    def issue_qz(ghq):
                        g2, hq2 = divmod(ghq, 16)
                        e0q2 = 64 * (hq2 % 2)
                        pkq2 = hq2 // 2
                        prevq2 = qz_hist.get(hq2 % 2)
                        if prevq2:
                            Y.wait_ge(srdy.h, prevq2)
                        bi2 = Y.dma_start(
                            out=qz[hq2 % 2][e0q2:e0q2 + 64, :],
                            in_=qtF[e0q2:e0q2 + 64, 2048 * pkq2 + 512 * g2:2048 * pkq2 + 512 * (g2 + 1)])
                        qzdd[hq2 % 2].inc(bi2, 16)

                    issue_qz(0)
                    cbuf = 0
                    acnt = 0
                    for g in range(4):
                        qis = tuple(4 * g + j for j in range(4))
                        ntile = 4 * g + 4
                        nchunks = (ntile + 1) // 2
                        used_this_g = set()
                        for j in range(2):
                            qi = qis[j]
                            if g == 0 and j == 0:
                                p1dn.wait(Y)
                            if g >= 1:
                                C.wait_ge(vD.h, 4 * g)
                                Y.wait_ge(x1dn.h, 4 * (g - 1) + j + 3)
                            bi = C.dma_start(out=h1r[j][:], in_=h1tok_d.ap()[128 * qi:128 * (qi + 1), :])
                            dh1r[j].inc(bi, 16)
                            bi = Y.dma_start(out=x0r[j][:], in_=x0_d.ap()[128 * qi:128 * (qi + 1), :])
                            dx0r[j].inc(bi, 16)

                        chunk_rec = {}
                        av_first = {}
                        for hh in range(H + 2):
                            if hh < H:
                                recs = []
                                av_first[hh] = True
                                if 16 * g + hh + 1 < 64:
                                    issue_qz(16 * g + hh + 1)
                            for ci in range(nchunks):
                                if hh < H:
                                    h = hh
                                    pk = h // 2
                                    e0 = 64 * (h % 2)
                                    t_lo = 2 * ci
                                    t_hi = min(ntile, 2 * (ci + 1))
                                    buf = cbuf % 3
                                    aslot = acnt % NATR
                                    cbuf += 1; acnt += 1
                                    prev = sc_hist.get(buf)
                                    if prev:
                                        T.wait_ge(expd.h, prev)
                                    prev_av = atr_hist.get(aslot)
                                    if prev_av:
                                        C.wait_ge(avd.h, prev_av)
                                    if ci == 0:
                                        qzdd[h % 2].wait(T)
                                    for t2 in range(t_lo, t_hi):
                                        jj = t2 - t_lo
                                        T.matmul(sC[buf][:, 512 * jj:512 * (jj + 1)],
                                                 kt[:, 2048 * pk + 128 * t2:2048 * pk + 128 * (t2 + 1)],
                                                 qz[h % 2][:, :],
                                                 start=True, stop=True,
                                                 skip_group_check=True)
                                    srdy.inc(T.drain())
                                    srdy.wait(C)
                                    C.activation(out=atr[aslot][:, 0:512 * (t_hi - t_lo)],
                                                 in_=sC[buf][:, 0:512 * (t_hi - t_lo)],
                                                 func=AF.Exp, scale=0.125, bias=km20[:, 0:1])
                                    expd.inc(C.drain())
                                    sc_hist[buf] = expd.n
                                    diag = [t2 for t2 in range(t_lo, t_hi) if t2 in qis]
                                    mskn = None
                                    if diag:
                                        expd.wait(V)
                                        for t2 in diag:
                                            jj = t2 - t_lo
                                            k = t2 - 4 * g
                                            V.tensor_tensor(
                                                out=atr[aslot][:, 512 * jj + 128 * k:512 * jj + 128 * (k + 1)],
                                                in0=atr[aslot][:, 512 * jj + 128 * k:512 * jj + 128 * (k + 1)],
                                                in1=mask[:], op=ALU.mult)
                                        mskd.inc(V.drain())
                                        mskn = mskd.n
                                    recs.append((aslot, t_lo, t_hi, expd.n, diag, mskn))
                                if 1 <= hh <= H:
                                    h = hh - 1
                                    gh = 16 * g + h
                                    (aslot, t_lo, t_hi, expn, diag, mskn) = chunk_rec[h][ci]
                                    if av_first[h]:
                                        prevn = nrm_hist.get(gh - 1)
                                        if prevn:
                                            T.wait_ge(nrm.h, prevn)
                                    T.wait_ge(expd.h, expn)
                                    for j2, qi in enumerate(qis):
                                        for t2 in range(t_lo, min(t_hi, qi + 1)):
                                            if t2 == qi:
                                                continue
                                            jj = t2 - t_lo
                                            T.matmul(oPd[h % 2][:, 128 * j2:128 * j2 + 65],
                                                     atr[aslot][:, 512 * jj + 128 * j2:512 * jj + 128 * (j2 + 1)],
                                                     vex[:, 1040 * t2 + 65 * h:1040 * t2 + 65 * h + 65],
                                                     start=av_first[h], stop=False, skip_group_check=True)
                                            av_first[h] = False
                                    if diag:
                                        T.wait_ge(mskd.h, mskn)
                                        for t2 in diag:
                                            j2 = t2 - 4 * g
                                            jj = t2 - t_lo
                                            T.matmul(oPd[h % 2][:, 128 * j2:128 * j2 + 65],
                                                     atr[aslot][:, 512 * jj + 128 * j2:512 * jj + 128 * (j2 + 1)],
                                                     vex[:, 1040 * t2 + 65 * h:1040 * t2 + 65 * h + 65],
                                                     start=av_first[h], stop=False, skip_group_check=True)
                                            av_first[h] = False


                            if hh < H:
                                chunk_rec[hh] = recs
                                qz_hist[hh % 2] = srdy.n
                            if 1 <= hh <= H:
                                h = hh - 1
                                gh = 16 * g + h
                                avd.inc(T.drain())
                                for (aslot, _, _, _, _, _) in chunk_rec[h]:
                                    atr_hist[aslot] = avd.n
                                avd.wait(V)
                                ob0 = 256 * (h % 2)
                                for j2 in range(4):
                                    V.reciprocal(linv[h % 2][:, j2:j2 + 1], oPd[h % 2][:, 128 * j2 + 64:128 * j2 + 65])
                                V.drain()
                                for j2 in range(4):
                                    V.tensor_scalar(out=osb1[:, ob0 + 64 * j2:ob0 + 64 * (j2 + 1)],
                                                    in0=oPd[h % 2][:, 128 * j2:128 * j2 + 64],
                                                    scalar1=linv[h % 2][:, j2:j2 + 1], scalar2=None, op0=ALU.mult)
                                nrm.inc(V.drain())
                                nrm_hist[gh] = nrm.n
                            if 2 <= hh:
                                h = hh - 2
                                gh = 16 * g + h
                                pk = h // 2
                                e0 = 64 * (h % 2)
                                ob0 = 256 * (h % 2)
                                if h == 0 and g >= 1:
                                    V.wait_ge(wod.h, 8 * g)
                                obv = osb1[:, ob0:ob0 + 256].rearrange("p (j b c) -> p j b c", j=4, b=2)
                                for bi2 in range(4):
                                    for bj in range(2):
                                        otv = hTb[e0 + 32 * bj:e0 + 32 * bj + 32, :] \
                                            .rearrange("p (j x) -> p j x", j=4)[:, :, 128 * pk + 32 * bi2:128 * pk + 32 * bi2 + 32]
                                        V.transpose(out=otv,
                                                    in_=obv[32 * bi2:32 * bi2 + 32, :, bj, :])
                                hTd.inc(V.drain())
                                hT_hist[gh] = hTd.n

                        hTd.wait(T)
                        for j, qi in enumerate(qis):
                            for half in range(2):
                                if wod.n >= 1:
                                    T.wait_ge(yadd.h, wod.n)
                                for c in range(NCH):
                                    T.matmul(woP[:, 0:512],
                                             hTb[:, 1024 * j + 128 * c:1024 * j + 128 * (c + 1)],
                                             wosb[:, 1024 * c + 512 * half:1024 * c + 512 * (half + 1)],
                                             start=(c == 0), stop=(c == NCH - 1),
                                             skip_group_check=True)
                                wod.inc(T.drain())
                                wod.wait(V)
                                if half == 0:
                                    V.wait_ge(dh1r[j % 2].h, dh1r[j % 2].n)
                                V.tensor_tensor(out=yt[:, 512 * half:512 * (half + 1)], in0=woP[:, 0:512],
                                                in1=h1r[j % 2][:, 512 * half:512 * (half + 1)], op=ALU.add)
                                yadd.inc(V.drain())
                            vD.inc(V.drain())
                            V.bn_stats(st2[:, 0:6], yt[:, 0:512]); V.drain()
                            V.bn_stats(st2[:, 6:12], yt[:, 512:1024]); V.drain()
                            V.bn_aggr(ag2[:], st2[:]); V.drain()
                            V.tensor_scalar(out=ve2[:], in0=ag2[:, 1:2], scalar1=EPS2, scalar2=None, op0=ALU.add)
                            yLd.inc(V.drain())
                            yLd.wait(G)
                            G.tensor_tensor(out=vi2[:], in0=ve2[:], in1=kmhalf[:], op=ALU.pow)
                            gLd.inc(G.drain())
                            gLd.wait(V)
                            V.tensor_scalar(out=z2[:], in0=yt[:], scalar1=ag2[:, 0:1], scalar2=vi2[:],
                                            op0=ALU.subtract, op1=ALU.mult)
                            V.drain()
                            V.tensor_tensor(out=z2[:], in0=z2[:], in1=gmt[:], op=ALU.mult); V.drain()
                            V.tensor_tensor(out=ab[:], in0=z2[:], in1=btt[:], op=ALU.add); V.drain()
                            par = qi % 2
                            if qi >= 2:
                                V.wait_ge(dx1s[par].h, 16 * (qi // 2))
                            V.tensor_tensor(out=ab[:], in0=ab[:], in1=gav[:], op=ALU.mult); V.drain()
                            V.wait_ge(dx0r[j % 2].h, dx0r[j % 2].n)
                            V.tensor_tensor(out=x1t[par][:], in0=x0r[j % 2][:], in1=ab[:], op=ALU.add)
                            x1dn.inc(V.drain())
                            x1dn.wait(Y)
                            bi = Y.dma_start(
                                out=x1send_h[qi // 8].ap()[qi % 8, :, :],
                                in_=x1t[par][:])
                            dx1s[par].inc(bi, 16)
                            if j + 2 < 4:
                                qn = qis[j + 2]
                                C.wait_ge(vD.h, vD.n)
                                bi = C.dma_start(out=h1r[j % 2][:], in_=h1tok_d.ap()[128 * qn:128 * (qn + 1), :])
                                dh1r[j % 2].inc(bi, 16)
                                Y.wait_ge(x1dn.h, x1dn.n)
                                bi = Y.dma_start(out=x0r[j % 2][:], in_=x0_d.ap()[128 * qn:128 * (qn + 1), :])
                                dx0r[j % 2].inc(bi, 16)

                        if g == 1:
                            from concourse.aot_env import allow_ici_all_to_all
                            G.wait_ge(dx1s[0].h, 64); G.wait_ge(dx1s[1].h, 64)
                            with allow_ici_all_to_all():
                                bi = G.collective_compute(
                                    "AllToAll", ALU.bypass,
                                    replica_groups=[list(range(8))],
                                    ins=[x1send_h[0].ap()], outs=[x1recv_h[0].ap()],
                                )
                            ccs.inc(bi, 1)

                pE.inc(T.drain())
                if dbg:
                    dx1s[0].wait(G); dx1s[1].wait(G)
                    for hh in range(2):
                        bi = G.dma_start(
                            out=dbg_x1.ap()[1024 * hh:1024 * (hh + 1), :],
                            in_=x1send_h[hh].ap().rearrange("b r d -> (b r) d")[:, :])
                        dgw.inc(bi, 16)

            from concourse.aot_env import allow_ici_all_to_all
            dx1s[0].wait(G); dx1s[1].wait(G)
            with allow_ici_all_to_all():
                bi = G.collective_compute(
                    "AllToAll", ALU.bypass,
                    replica_groups=[list(range(8))],
                    ins=[x1send_h[1].ap()], outs=[x1recv_h[1].ap()],
                )
            ccs2.inc(bi, 1)

            h2rdy = sm("h2rdy"); xfd = sm("xfd"); hevc = sm("hevc")
            grd = sm("grd"); gevc = sm("gevc"); cupd = sm("cupd")
            bcd = sm("bcd"); bce = sm("bce"); zfd = sm("zfd")
            lnf = sm("lnf"); glnf = sm("glnf"); c0d = sm("c0d")

            with contextlib.ExitStack() as ffA:
                sbf = lambda n, sh, d=BF16: ffA.enter_context(nc.sbuf_tensor(n, sh, d))
                h2T = sbf("h2T", [128, B * 2048])
                cst = sbf("cst", [128, 2048])
                uoo = sbf("uoo", [128, 2048])
                gbt = [sbf(f"gbt{i}", [128, D]) for i in range(2)]

                with contextlib.ExitStack() as ffB:
                    sbb = lambda n, sh, d=BF16: ffB.enter_context(nc.sbuf_tensor(n, sh, d))
                    gwt = {}
                    for nm in ("wfT", "ufT", "wiT", "uiT", "wcT", "ucT"):
                        gwt[nm] = sbb(nm + "sb", [128, 8192])
                    bp0 = [sbb(f"bp0{i}", [1, D]) for i in range(3)]
                    cb1 = [sbb(f"cb1{i}", [128, D]) for i in range(2)]
                    sbt = [sbb(f"sbt{i}", [128, D]) for i in range(2)]
                    fsb = sbb("fsb", [128, 2048])
                    isb = sbb("isb", [128, 2048])
                    gsb = sbb("gsb", [128, 2048])
                    x1s = [sbb(f"x1s{i}", [128, D], F32) for i in range(2)]
                    zf = sbb("zf", [128, D])
                    h2tok = [sbb(f"h2tok{i}", [128, D]) for i in range(2)]

                    pE.wait(G)
                    G.wait_ge(x1dn.h, x1dn.n)
                    G.wait_ge(hTd.h, hTd.n)
                    dgr = sm("dgr")
                    for i, r in enumerate((0, 1, 2)):
                        bi = G.dma_start(out=bp0[i][:], in_=brow_d.ap()[r:r + 1, :])
                        dgr.inc(bi, 16)
                    C.wait_ge(pC.h, pC.n)
                    bi = G.dma_start(out=mp0a[:], in_=brow_d.ap()[4:5, :])
                    dgr.inc(bi, 16)
                    dgf3 = [sm(f"dgf3{i}") for i in range(3)]
                    for gi2, pair in enumerate((("wfT", wfT_d, "ufT", ufT_d),
                                                ("wiT", wiT_d, "uiT", uiT_d),
                                                ("wcT", wcT_d, "ucT", ucT_d))):
                        for nm, srcd in (pair[0:2], pair[2:4]):
                            bi = G.dma_start(out=gwt[nm][:].rearrange("p (c n) -> p c n", c=8),
                                             in_=srcd.ap().rearrange("(c p) n -> p c n", p=128))
                            dgf3[gi2].inc(bi, 16)

                    xpf = [ffB.enter_context(nc.psum_tensor(f"xpf{i}", [128, 1024], BF16)) for i in range(2)]
                    bpf = ffB.enter_context(nc.psum_tensor("bpf", [128, 512], F32))
                    rg = ffB.enter_context(nc.psum_tensor("Rg", [128, 2048], F32))

                    def ff_bcast(dstt, srcrow, guard=None):
                        for hh in range(2):
                            bce.wait(T)
                            T.matmul(bpf[:, 0:512], onesr[0:1, 0:128],
                                     srcrow[0:1, 512 * hh:512 * (hh + 1)], start=True, stop=True)
                            bcd.inc(T.drain())
                            bcd.wait(C)
                            if guard is not None:
                                C.wait_ge(guard[0].h, guard[1])
                            C.copy(out=dstt[:, 512 * hh:512 * (hh + 1)], in_=bpf[:, 0:512])
                            bce.inc(C.drain())

                    dgr.wait(T)
                    for mi in range(NCH):
                        bce.wait(T)
                        T.matmul(bpf[:, 0:256], mp0a[0:1, 128 * mi:128 * (mi + 1)], onesr[0:1, 0:256],
                                 start=True, stop=True)
                        c0d.inc(T.drain())
                        c0d.wait(C)
                        C.copy(out=cst[:, 256 * mi:256 * (mi + 1)], in_=bpf[:, 0:256])
                        bce.inc(C.drain())

                    ccs.wait(Y)
                    c0d.wait(C)
                    def prep_bcast(t):
                        tp2 = t % 2
                        if t >= 2:
                            C.wait_ge(h2rdy.h, 2 * t - 2)
                        C.wait_ge(bcd.h, bcd.n)
                        bi2 = C.dma_start(out=mp0a[:], in_=mod[t:t + 1, 4 * D:5 * D]); dscq.inc(bi2, 16)
                        bi2 = C.dma_start(out=mp0b[:], in_=mod[t:t + 1, 3 * D:4 * D]); dscq.inc(bi2, 16)
                        dscq.wait(T)
                        ff_bcast(cb1[tp2], mp0a)
                        ff_bcast(sbt[tp2], mp0b)
                    def prep_ln(t, rt):
                        tp = t % 2
                        it = 2 * t + rt
                        sl = it % 2
                        if it >= 2:
                            Y.wait_ge(zfd.h, it - 1)
                        if rt == 1 and t == 0:
                            ccs2.wait(Y)
                        bi = Y.dma_start(out=x1s[sl][:], in_=x1recv_h[rt].ap()[t, :, :])
                        dxr[sl].inc(bi, 16)
                        V.wait_ge(dxr[sl].h, dxr[sl].n)
                        V.bn_stats(st1[:, 0:6], x1s[sl][:, 0:512]); V.drain()
                        V.bn_stats(st1[:, 6:12], x1s[sl][:, 512:1024]); V.drain()
                        V.bn_aggr(ag1[:], st1[:]); V.drain()
                        V.tensor_scalar(out=ve1[:], in0=ag1[:, 1:2], scalar1=EPS1, scalar2=None, op0=ALU.add)
                        lnf.inc(V.drain())
                        lnf.wait(G)
                        G.tensor_tensor(out=vi1[:], in0=ve1[:], in1=kmhalf[:], op=ALU.pow)
                        glnf.inc(G.drain())
                        glnf.wait(V)
                        V.tensor_scalar(out=zf[:], in0=x1s[sl][:], scalar1=ag1[:, 0:1], scalar2=vi1[:],
                                        op0=ALU.subtract, op1=ALU.mult)
                        zfd.inc(V.drain())
                        V.wait_ge(bce.h, 8 + 4 * (t + 1))
                        if it >= 2:
                            V.wait_ge(xfd.h, it - 1)
                        V.tensor_tensor(out=h2tok[sl][:], in0=zf[:], in1=cb1[tp][:], op=ALU.mult); V.drain()
                        V.tensor_tensor(out=h2tok[sl][:], in0=h2tok[sl][:], in1=sbt[tp][:], op=ALU.add)
                        h2rdy.inc(V.drain())

                    def prep_tr(t, rt):
                        it = 2 * t + rt
                        sl = it % 2
                        T.wait_ge(h2rdy.h, it + 1)
                        if it >= 2:
                            T.wait_ge(hevc.h, it - 1)
                        for c in range(NCH):
                            T.transpose(xpf[sl][:, 128 * c:128 * (c + 1)], h2tok[sl][:, 128 * c:128 * (c + 1)], ident[:])
                        xfd.inc(T.drain())
                        xfd.wait(C)
                        C.copy(out=h2T[:].rearrange("p (t c r) -> p t c r", t=B, c=NCH)
                               [:, t, :, 128 * rt:128 * (rt + 1)],
                               in_=xpf[sl][:].rearrange("p (c r) -> p c r", c=NCH))
                        hevc.inc(C.drain())

                    gates = (("wfT", "ufT", 0, AF.Sigmoid, fsb), ("wiT", "uiT", 1, AF.Sigmoid, isb),
                             ("wcT", "ucT", 2, AF.Tanh, gsb))

                    def gate_step(t):
                        for gi, (wn, un, bidx, fn, dstt) in enumerate(gates):
                            if t == 0:
                                T.wait_ge(dgf3[gi].h, 32)
                                if gi == 0:
                                    T.wait_ge(bce.h, 8)
                            gevc.wait(T)
                            T.wait_ge(hevc.h, 2 * (t + 1))
                            T.wait_ge(cupd.h, t)
                            for bank in range(4):
                                for k in range(2):
                                    mi = 2 * bank + k
                                    T.matmul(rg[:, 256 * mi:256 * (mi + 1)],
                                             bp0[bidx][0:1, 128 * mi:128 * (mi + 1)], onesr[0:1, 0:256],
                                             start=(k == 0), stop=False, skip_group_check=True)
                                for k in range(2):
                                    mi = 2 * bank + k
                                    for c in range(NCH):
                                        T.matmul(rg[:, 256 * mi:256 * (mi + 1)],
                                                 gwt[wn][:, 1024 * c + 128 * mi:1024 * c + 128 * (mi + 1)],
                                                 h2T[:, 2048 * t + 256 * c:2048 * t + 256 * (c + 1)],
                                                 start=False, stop=False, skip_group_check=True)
                                for k in range(2):
                                    mi = 2 * bank + k
                                    for c in range(NCH):
                                        T.matmul(rg[:, 256 * mi:256 * (mi + 1)],
                                                 gwt[un][:, 1024 * c + 128 * mi:1024 * c + 128 * (mi + 1)],
                                                 cst[:, 256 * c:256 * (c + 1)],
                                                 start=False, stop=(k == 1 and c == NCH - 1),
                                                 skip_group_check=True)
                            grd.inc(T.drain())
                            grd.wait(C)
                            C.activation(out=dstt[:], in_=rg[:, 0:2048], func=fn)
                            gevc.inc(C.drain())
                        gevc.wait(V)
                        V.tensor_tensor(out=fsb[:], in0=fsb[:], in1=cst[:], op=ALU.mult); V.drain()
                        V.tensor_tensor(out=isb[:], in0=isb[:], in1=gsb[:], op=ALU.mult); V.drain()
                        V.tensor_tensor(out=cst[:], in0=fsb[:], in1=isb[:], op=ALU.add)
                        cupd.inc(V.drain())

                    prep_bcast(0); prep_bcast(1)
                    prep_ln(0, 0); prep_ln(0, 1)
                    prep_tr(0, 0); prep_tr(0, 1)
                    prep_bcast(2)
                    prep_ln(1, 0); prep_ln(1, 1)
                    for t in range(B):
                        gate_step(t)
                        if t + 1 < B:
                            prep_tr(t + 1, 0); prep_tr(t + 1, 1)
                        if t + 3 < B:
                            prep_bcast(t + 3)
                        if t + 2 < B:
                            prep_ln(t + 2, 0); prep_ln(t + 2, 1)
                    if dbg:
                        G.wait_ge(hevc.h, hevc.n)
                        bi = G.dma_start(out=dbg_h2.ap()[:, :], in_=h2T[:, :])
                        dgf.inc(bi, 16)
                    if dbg:
                        G.wait_ge(cupd.h, cupd.n)
                        bi = G.dma_start(out=dbg_c.ap()[:, :], in_=cst[:, :])
                        dgf.inc(bi, 16)

                with contextlib.ExitStack() as ffC:
                    sbc = lambda n, sh, d=BF16: ffC.enter_context(nc.sbuf_tensor(n, sh, d))
                    woTs = sbc("woTs", [128, 8192])
                    uoTs = sbc("uoTs", [128, 8192])
                    bp0o = sbc("bp0o", [1, D])
                    gbt8 = [sbc(f"gbt8{i}", [128, D]) for i in range(B)]
                    x1f = [sbc(f"x1f{i}", [128, D], F32) for i in range(2)]
                    tmpf = [sbc(f"tmpf{i}", [128, D], F32) for i in range(2)]
                    hb = [sbc(f"hb{i}", [128, D]) for i in range(2)]
                    outt = [sbc(f"outt{i}", [128, D], F32) for i in range(2)]
                    wps = [ffC.enter_context(nc.psum_tensor(f"wps{i}", [128, 1024], F32)) for i in range(2)]
                    bpo = ffC.enter_context(nc.psum_tensor("bpo", [128, 512], F32))

                    uod = sm("uod"); uoe = sm("uoe"); wpd = sm("wpd")
                    fvd = sm("fvd"); ftd = sm("ftd"); odn = sm("odn")
                    fbd = sm("fbd"); fbe = sm("fbe")

                    G.wait_ge(cupd.h, cupd.n)
                    G.wait_ge(gevc.h, gevc.n)
                    dgu = sm("dgu")
                    bi = G.dma_start(out=bp0o[:], in_=brow_d.ap()[3:4, :])
                    dgu.inc(bi, 16)
                    bi = G.dma_start(out=uoTs[:].rearrange("p (c n) -> p c n", c=8),
                                     in_=uoT_d.ap().rearrange("(c p) n -> p c n", p=128))
                    dgu.inc(bi, 16)
                    bi = G.dma_start(out=woTs[:].rearrange("p (c n) -> p c n", c=8),
                                     in_=woT_d.ap().rearrange("(c p) n -> p c n", p=128))
                    dgf.inc(bi, 16)

                    def fin_bcast(dstt, srcrow):
                        for hh in range(2):
                            fbe.wait(T)
                            T.matmul(bpo[:, 0:512], onesr[0:1, 0:128],
                                     srcrow[0:1, 512 * hh:512 * (hh + 1)], start=True, stop=True)
                            fbd.inc(T.drain())
                            fbd.wait(C)
                            C.copy(out=dstt[:, 512 * hh:512 * (hh + 1)], in_=bpo[:, 0:512])
                            fbe.inc(C.drain())

                    dgu.wait(T)
                    cupd.wait(T)
                    cupd.wait(Y)
                    for rt in range(2):
                        for half in range(2):
                            T.matmul(wps[rt][:, 512 * half:512 * (half + 1)],
                                     onesr[0:1, 0:128], bp0o[0:1, 512 * half:512 * (half + 1)],
                                     start=True, stop=False, skip_group_check=True)
                            for c in range(NCH):
                                T.matmul(wps[rt][:, 512 * half:512 * (half + 1)],
                                         cst[:, 256 * c + 128 * rt:256 * c + 128 * (rt + 1)],
                                         uoTs[:, 1024 * c + 512 * half:1024 * c + 512 * (half + 1)],
                                         start=False, stop=(c == NCH - 1), skip_group_check=True)
                        uod.inc(T.drain())
                        uod.wait(C)
                        C.copy(out=uoo[:, 1024 * rt:1024 * (rt + 1)], in_=wps[rt][:, 0:1024])
                        uoe.inc(C.drain())

                    def gb_build(t):
                        C.wait_ge(fbd.h, fbd.n)
                        bi = C.dma_start(out=mp0b[:], in_=mod[t:t + 1, 5 * D:6 * D]); dscq.inc(bi, 16)
                        dscq.wait(T)
                        fin_bcast(gbt8[t], mp0b)

                    gb_build(0); gb_build(1)
                    for t in range(B):
                        for rt in range(2):
                            it = 2 * t + rt
                            sl = it % 2
                            T.wait_ge(uoe.h, 2)
                            if it == 0:
                                dgf.wait(T)
                            if it >= 2:
                                T.wait_ge(fvd.h, it - 1)
                            for half in range(2):
                                for c in range(NCH):
                                    T.matmul(wps[sl][:, 512 * half:512 * (half + 1)],
                                             h2T[:, 2048 * t + 256 * c + 128 * rt:2048 * t + 256 * c + 128 * (rt + 1)],
                                             woTs[:, 1024 * c + 512 * half:1024 * c + 512 * (half + 1)],
                                             start=(c == 0), stop=(c == NCH - 1), skip_group_check=True)
                            wpd.inc(T.drain())
                            if rt == 1 and t + 2 < B:
                                gb_build(t + 2)
                            if it >= 2:
                                Y.wait_ge(odn.h, it - 1)
                            bi = Y.dma_start(out=x1f[sl][:], in_=x1recv_h[rt].ap()[t, :, :])
                            dxf.inc(bi, 16)
                            wpd.wait(V)
                            V.wait_ge(uoe.h, 2)
                            if it >= 2:
                                V.wait_ge(ftd.h, it - 1)
                            V.tensor_tensor(out=tmpf[sl][:], in0=wps[sl][:, 0:1024],
                                            in1=uoo[:, 1024 * rt:1024 * (rt + 1)], op=ALU.add)
                            fvd.inc(V.drain())
                            fvd.wait(C)
                            C.activation(out=hb[sl][:], in_=tmpf[sl][:], func=AF.Tanh)
                            ftd.inc(C.drain())
                            ftd.wait(V)
                            V.wait_ge(fbe.h, 2 * (t + 1))
                            V.tensor_tensor(out=hb[sl][:], in0=hb[sl][:], in1=gbt8[t][:], op=ALU.mult); V.drain()
                            V.wait_ge(dxf.h, 16 * (it + 1))
                            if it >= 2:
                                V.wait_ge(dout[sl].h, 16 * (it // 2))
                            V.tensor_tensor(out=outt[sl][:], in0=x1f[sl][:], in1=hb[sl][:], op=ALU.add)
                            odn.inc(V.drain())
                            odn.wait(Y)
                            bi = Y.dma_start(out=out_d.ap()[t, 128 * rt:128 * (rt + 1), :], in_=outt[sl][:])
                            dout[sl].inc(bi, 16)

                    Y.wait_ge(dout[0].h, dout[0].n)
                    Y.wait_ge(dout[1].h, dout[1].n)
                    if dbg:
                        G.wait_ge(dgf.h, dgf.n)
    return nc


_NC_CACHE = {}


def _get_nc(dbg=False):
    key = bool(dbg)
    if key not in _NC_CACHE:
        _NC_CACHE[key] = build_nc(dbg=key)
    return _NC_CACHE[key]


def make_in_maps(inputs):
    x = np.ascontiguousarray(np.asarray(inputs["L0_attn_x"]), dtype=np.float32)
    c = np.ascontiguousarray(np.asarray(inputs["L0_c"]), dtype=np.float32)
    wmod = np.ascontiguousarray(np.asarray(inputs["L0_wmod"]), dtype=np.float32)
    wq = np.ascontiguousarray(np.asarray(inputs["L0_attn_Wq"]).transpose(1, 0, 2).reshape(D, D), dtype=np.float32)
    wk = np.ascontiguousarray(np.asarray(inputs["L0_attn_Wk"]).transpose(1, 0, 2).reshape(D, D), dtype=np.float32)
    wv = np.ascontiguousarray(np.asarray(inputs["L0_attn_Wv"]).transpose(1, 0, 2).reshape(D, D), dtype=np.float32)
    wo = np.ascontiguousarray(np.asarray(inputs["L0_attn_Wo"]), dtype=np.float32)
    gam = np.ascontiguousarray(np.asarray(inputs["L0_attn_gamma"]), dtype=np.float32).reshape(1, D)
    bet = np.ascontiguousarray(np.asarray(inputs["L0_attn_beta"]), dtype=np.float32).reshape(1, D)
    trs = lambda k: np.ascontiguousarray(np.asarray(inputs[k], dtype=np.float32).T)
    wfT, ufT = trs("L0_ffn_wf"), trs("L0_ffn_uf")
    wiT, uiT = trs("L0_ffn_wi"), trs("L0_ffn_ui")
    wcT, ucT = trs("L0_ffn_wc"), trs("L0_ffn_uc")
    woT, uoT = trs("L0_ffn_wo"), trs("L0_ffn_uo")
    brows = np.ascontiguousarray(np.stack([
        np.asarray(inputs["L0_ffn_bf"], dtype=np.float32),
        np.asarray(inputs["L0_ffn_bi"], dtype=np.float32),
        np.asarray(inputs["L0_ffn_bc"], dtype=np.float32),
        np.asarray(inputs["L0_ffn_bo"], dtype=np.float32),
        np.asarray(inputs["L0_ffn_in_c"], dtype=np.float32),
    ]))
    ident = np.eye(128, dtype=BF)
    masktile = np.triu(np.ones((128, 128), dtype=np.float32)).astype(BF)
    onesrow = np.ones((1, 512), dtype=BF)
    shared = dict(wmod=wmod, wq=wq, wk=wk, wv=wv, wo=wo, gamma=gam, beta=bet,
                  wfT=wfT, ufT=ufT, wiT=wiT, uiT=uiT, wcT=wcT, ucT=ucT,
                  woT=woT, uoT=uoT, brows=brows, ident=ident,
                  masktile=masktile, onesrow=onesrow)
    in_maps = []
    for b in range(B):
        m = dict(shared)
        m["x0"] = np.ascontiguousarray(x[b])
        m["c9"] = np.ascontiguousarray(np.concatenate([c, c[b:b + 1]], axis=0))
        in_maps.append(m)
    return in_maps


def kernel(**inputs):
    nc = _get_nc(dbg=False)
    in_maps = make_in_maps(inputs)
    res = run_bass_kernel_spmd(nc, in_maps, core_ids=list(range(8)))
    full = np.empty((B, S, D), dtype=np.float32)
    for j in range(8):
        o = np.asarray(res.results[j]["out"])
        full[:, 128 * j:128 * (j + 1), :] = o[:, 0:128, :]
        full[:, 1024 + 128 * j:1024 + 128 * (j + 1), :] = o[:, 128:256, :]
    return full


# version 26
# speedup vs baseline: 1.0450x; 1.0450x over previous
import os
os.environ.setdefault("TRN_ALLOW_ICI_ALL_TO_ALL", "1")

import contextlib
import numpy as np
import ml_dtypes

import concourse.bass as bass
import concourse.mybir as mybir
from concourse.bass_utils import run_bass_kernel_spmd

F32 = mybir.dt.float32
BF16 = mybir.dt.bfloat16
AF = mybir.ActivationFunctionType
ALU = mybir.AluOpType
BF = ml_dtypes.bfloat16

D = 1024
S = 2048
B = 8
H = 16
E = 64
NT = 16
NCH = 8
NPK = 8
R = 256
EPS1 = 1e-5
EPS2 = 1e-3
EXPB = -20.0


class Sem:
    def __init__(self, nc, es, name):
        self.h = es.enter_context(nc.semaphore(name))
        self.n = 0

    def inc(self, bi, k=1):
        bi.then_inc(self.h, k)
        self.n += k

    def wait(self, eng, n=None):
        v = self.n if n is None else n
        if v > 0:
            eng.wait_ge(self.h, v)


def build_nc(dbg=False):
    nc = bass.Bass("TRN2", target_bir_lowering=False)

    dt = lambda n, sh, k="ExternalInput", d=F32: nc.dram_tensor(n, sh, d, kind=k)
    x0_d = dt("x0", [S, D])
    c9_d = dt("c9", [9, D])
    wmod_d = dt("wmod", [D, 6 * D])
    wq_d = dt("wq", [D, D]); wk_d = dt("wk", [D, D]); wv_d = dt("wv", [D, D])
    wo_d = dt("wo", [D, D])
    gam_d = dt("gamma", [1, D]); bet_d = dt("beta", [1, D])
    wfT_d = dt("wfT", [D, D]); ufT_d = dt("ufT", [D, D])
    wiT_d = dt("wiT", [D, D]); uiT_d = dt("uiT", [D, D])
    wcT_d = dt("wcT", [D, D]); ucT_d = dt("ucT", [D, D])
    woT_d = dt("woT", [D, D]); uoT_d = dt("uoT", [D, D])
    brow_d = dt("brows", [5, D])
    ident_d = dt("ident", [128, 128], d=BF16)
    mask_d = dt("masktile", [128, 128], d=BF16)
    ones_d = dt("onesrow", [1, 512], d=BF16)
    out_d = dt("out", [B, R, D], k="ExternalOutput")
    x1send_h = [nc.dram_tensor("x1sendA", [B, 128, D], F32),
                nc.dram_tensor("x1sendB", [B, 128, D], F32)]
    h1tok_d = nc.dram_tensor("h1tok", [S, D], BF16)
    x1recv_h = [nc.dram_tensor("x1recvA", [B, 128, D], F32),
                nc.dram_tensor("x1recvB", [B, 128, D], F32)]
    if dbg:
        dbg_mod = nc.dram_tensor("dbg_mod", [9, 6 * D], F32, kind="ExternalOutput")
        dbg_x1 = nc.dram_tensor("dbg_x1", [S, D], F32, kind="ExternalOutput")
        dbg_c = nc.dram_tensor("dbg_c", [128, 2048], F32, kind="ExternalOutput")
        dbg_h2 = nc.dram_tensor("dbg_h2", [128, B * 2048], F32, kind="ExternalOutput")

    es = contextlib.ExitStack()
    with es:
        sb = lambda n, sh, d=BF16: es.enter_context(nc.sbuf_tensor(n, sh, d))
        ident = sb("identsb", [128, 128])
        mask = sb("masksb", [128, 128])
        onesr = sb("onesrsb", [1, 512])
        km20 = sb("km20", [128, 1], F32)
        kmhalf = sb("kmhalf", [128, 1], F32)
        mod = sb("modsb", [9, 6 * D])
        mp0a = sb("mp0a", [1, D])
        mp0b = sb("mp0b", [1, D])
        st1 = sb("st1", [128, 12], F32); ag1 = sb("ag1", [128, 2], F32)
        ve1 = sb("ve1", [128, 1], F32); vi1 = sb("vi1", [128, 1], F32)
        st2 = sb("st2", [128, 12], F32); ag2 = sb("ag2", [128, 2], F32)
        ve2 = sb("ve2", [128, 1], F32); vi2 = sb("vi2", [128, 1], F32)

        sm = lambda n: Sem(nc, es, n)
        dcon = sm("dcon"); dxr = [sm("dxr0"), sm("dxr1")]; dxf = sm("dxf"); dgwm = sm("dgwm")
        dx0s = [sm(f"dx0s{i}") for i in range(3)]
        dx1s = [sm(f"dx1s{i}") for i in range(2)]
        dout = [sm("dout0"), sm("dout1")]; dscq = sm("dscq"); dgw = sm("dgw"); dgf = sm("dgf")
        ccs = sm("ccs"); ccs2 = sm("ccs2")
        kons = sm("kons")

        T = nc.tensor; V = nc.vector; C = nc.scalar; G = nc.gpsimd; Y = nc.sync

        blk = es.enter_context(nc.Block())

        @blk.sync
        def _(sync):
            V.memset(km20[:], EXPB)
            V.memset(kmhalf[:], -0.5)
            kons.inc(V.drain())
            kons.wait(C)
            kons.wait(G)
            bi = Y.dma_start(out=ident[:], in_=ident_d.ap()[:, :]); dcon.inc(bi, 16)
            bi = Y.dma_start(out=mask[:], in_=mask_d.ap()[:, :]); dcon.inc(bi, 16)
            bi = Y.dma_start(out=onesr[:], in_=ones_d.ap()[:, :]); dcon.inc(bi, 16)

            svA = sm("svA"); spA = sm("spA"); spB = sm("spB")
            ssA = sm("ssA"); ssB = sm("ssB"); ssC = sm("ssC")

            with contextlib.ExitStack() as st_es:
                sbl = lambda n, sh, d=BF16: st_es.enter_context(nc.sbuf_tensor(n, sh, d))
                c9f = sbl("c9f", [9, D], F32)
                siluc = sbl("siluc", [9, D])
                silucT = sbl("silucT", [128, 9 * NCH])
                ghelp = sbl("ghelp", [9, 2 * D])
                cblk = sbl("cblk", [128, 8 * 6144])
                xps = st_es.enter_context(nc.psum_tensor("xps", [128, 128], BF16))
                mps = [st_es.enter_context(nc.psum_tensor(f"mps{i}", [9, 512], F32)) for i in range(2)]

                bi = Y.dma_start(out=c9f[:], in_=c9_d.ap()[:, :]); dcon.inc(bi, 16)
                wmv = wmod_d.ap().rearrange("(c p) n -> p c n", p=128)
                dgwm4 = [sm(f"dgwm4{i}") for i in range(4)]
                for s4 in range(4):
                    bi = G.dma_start(
                        out=cblk[:].rearrange("p (c n) -> p c n", c=8)[:, :, 1536 * s4:1536 * (s4 + 1)],
                        in_=wmv[:, :, 1536 * s4:1536 * (s4 + 1)])
                    dgwm4[s4].inc(bi, 16)

                dcon.wait(C)
                C.activation(out=siluc[:], in_=c9f[:], func=AF.Silu)
                ssA.inc(C.drain())
                ssA.wait(T)
                for c in range(NCH):
                    T.transpose(xps[0:128, 16 * c:16 * c + 9], siluc[0:9, 128 * c:128 * (c + 1)], ident[0:9, 0:9])
                spA.inc(T.drain())
                spA.wait(C)
                C.copy(out=silucT[:].rearrange("p (c n) -> p c n", c=NCH),
                       in_=xps[:].rearrange("p (c n) -> p c n", c=NCH)[:, :, 0:9])
                ssB.inc(C.drain())
                ssB.wait(T)
                for j in range(12):
                    T.wait_ge(dgwm4[j // 3].h, 16)
                    if j >= 2:
                        T.wait_ge(ssC.h, j - 1)
                    mp = mps[j % 2]
                    for c in range(NCH):
                        T.matmul(mp[0:9, 0:512], silucT[:, 9 * c:9 * (c + 1)],
                                 cblk[:, 6144 * c + 512 * j:6144 * c + 512 * (j + 1)],
                                 start=(c == 0), stop=(c == NCH - 1))
                    spB.inc(T.drain())
                    spB.wait(C)
                    C.copy(out=mod[0:9, 512 * j:512 * (j + 1)], in_=mp[0:9, 0:512])
                    ssC.inc(C.drain())
                ssC.wait(V)
                for lo in (D, 4 * D):
                    V.tensor_scalar(out=mod[0:9, lo:lo + D], in0=mod[0:9, lo:lo + D],
                                    scalar1=1.0, scalar2=None, op0=ALU.add)
                    V.drain()
                for lo in (2 * D, 5 * D):
                    sl = mod[0:9, lo:lo + D]
                    V.tensor_scalar(out=ghelp[0:9, 0:D], in0=sl, scalar1=0.0, scalar2=2.0,
                                    op0=ALU.is_ge, op1=ALU.mult)
                    V.drain()
                    V.tensor_scalar(out=ghelp[0:9, 0:D], in0=ghelp[0:9, 0:D], scalar1=-1.0,
                                    scalar2=None, op0=ALU.add)
                    V.drain()
                    V.tensor_scalar(out=ghelp[0:9, D:2 * D], in0=sl, scalar1=-1.0, scalar2=None,
                                    op0=ALU.mult)
                    V.drain()
                    V.tensor_tensor(out=ghelp[0:9, D:2 * D], in0=sl, in1=ghelp[0:9, D:2 * D], op=ALU.max)
                    V.drain()
                    V.tensor_scalar(out=ghelp[0:9, D:2 * D], in0=ghelp[0:9, D:2 * D], scalar1=0.25,
                                    scalar2=None, op0=ALU.max)
                    V.drain()
                    V.tensor_tensor(out=sl, in0=ghelp[0:9, 0:D], in1=ghelp[0:9, D:2 * D], op=ALU.mult)
                    V.drain()
                svA.inc(V.drain())
                if dbg:
                    svA.wait(G)
                    bi = G.dma_start(out=dbg_mod.ap()[:, :], in_=mod[0:9, :])
                    dgw.inc(bi, 16)

            pE = sm("pE"); x1dn = sm("x1dn"); hTd = sm("hTd")
            vh1 = sm("vh1"); xprdy = sm("xprdy"); h1wld = sm("h1wld")
            pjev = sm("pjev"); pjdn = sm("pjdn"); p1dn = sm("p1dn")
            dh1w = [sm("dh1w0"), sm("dh1w1")]; dh1r = [sm("dh1r0"), sm("dh1r1")]
            y1d = sm("y1d"); g1d = sm("g1d"); yLd = sm("yLd"); gLd = sm("gLd")
            vLN = sm("vLN")
            srdy = sm("srdy"); expd = sm("expd"); mskd = sm("mskd")
            avd = sm("avd"); nrm = sm("nrm"); oxp = sm("oxp")
            wod = sm("wod"); vD = sm("vD")
            pC = sm("pC"); sD = sm("sD"); vB = sm("vB")
            dx0r = [sm(f"dx0r{i}") for i in range(2)]

            with contextlib.ExitStack() as ao_es:
                sbo = lambda n, sh, d=BF16: ao_es.enter_context(nc.sbuf_tensor(n, sh, d))
                kt = sbo("ktsb", [128, NPK * S])
                vex = sbo("vexsb", [128, NT * (H * 65)])
                qtF = sbo("qtFsb", [128, NPK * S])
                wosb = ao_es.enter_context(nc.sbuf_tensor("wosb", [128, 8192], BF16, side="right"))

                with contextlib.ExitStack() as p1_es:
                    sb1 = lambda n, sh, d=BF16: p1_es.enter_context(nc.sbuf_tensor(n, sh, d))
                    sb1r = lambda n, sh, d=BF16: p1_es.enter_context(nc.sbuf_tensor(n, sh, d, side="right"))
                    wq = sb1r("wqsb", [128, 8192]); wk = sb1r("wksb", [128, 8192]); wv = sb1r("wvsb", [128, 8192])
                    h1wW = sb1("h1wW", [128, 8 * 512])
                    x0s = [sb1(f"x0s{i}", [128, D], F32) for i in range(3)]
                    hb1 = [sb1(f"hb1{i}", [128, D]) for i in range(2)]
                    ca1 = sb1("ca1sb", [128, D]); sav = sb1("savsb", [128, D])

                    for (dst, srcd) in ((wq, wq_d), (wk, wk_d), (wv, wv_d), (wosb, wo_d)):
                        bi = G.dma_start(out=dst[:].rearrange("p (c n) -> p c n", c=8),
                                         in_=srcd.ap().rearrange("(c p) n -> p c n", p=128))
                        dgw.inc(bi, 16)
                    V.wait_ge(spB.h, 12)
                    V.memset(vex[:].rearrange("p (s c) -> p s c", c=65)[:, :, 64:65], 1.0)
                    vB.inc(V.drain())

                    xpA = p1_es.enter_context(nc.psum_tensor("xpA", [128, 1024], BF16))
                    xpB = p1_es.enter_context(nc.psum_tensor("xpB", [128, 1024], BF16))
                    pj1 = p1_es.enter_context(nc.psum_tensor("pj1", [128, 512], F32))
                    pj2 = p1_es.enter_context(nc.psum_tensor("pj2", [128, 512], F32))
                    pjV = p1_es.enter_context(nc.psum_tensor("pjV", [128, 1024], F32))

                    dgw.wait(T)
                    dcon.wait(T)
                    svA.wait(C)
                    for dstt, lo in ((sav, 0), (ca1, D)):
                        C.wait_ge(pC.h, pC.n)
                        bi = C.dma_start(out=mp0a[:], in_=mod[8:9, lo:lo + D]); dscq.inc(bi, 16)
                        dscq.wait(T)
                        for hh in range(2):
                            sD.wait(T)
                            T.matmul(pj1[:, 0:512], onesr[0:1, 0:128],
                                     mp0a[0:1, 512 * hh:512 * (hh + 1)], start=True, stop=True)
                            pC.inc(T.drain())
                            pC.wait(C)
                            C.copy(out=dstt[:, 512 * hh:512 * (hh + 1)], in_=pj1[:, 0:512])
                            sD.inc(C.drain())

                    Y.wait_ge(spB.h, 12)
                    for k in range(2):
                        bi = Y.dma_start(out=x0s[k][:], in_=x0_d.ap()[128 * k:128 * (k + 1), :])
                        dx0s[k].inc(bi, 16)

                    sD.wait(V)
                    pj_hist = {}
                    for w in range(4):
                        for k in range(4):
                            ti = 4 * w + k
                            if ti + 2 < NT:
                                if ti >= 1:
                                    Y.wait_ge(vLN.h, ti)
                                bi = Y.dma_start(out=x0s[(ti + 2) % 3][:],
                                                 in_=x0_d.ap()[128 * (ti + 2):128 * (ti + 3), :])
                                dx0s[(ti + 2) % 3].inc(bi, 16)
                            dx0s[ti % 3].wait(V)
                            xs = x0s[ti % 3]
                            V.bn_stats(st1[:, 0:6], xs[:, 0:512]); V.drain()
                            V.bn_stats(st1[:, 6:12], xs[:, 512:1024]); V.drain()
                            V.bn_aggr(ag1[:], st1[:]); V.drain()
                            V.tensor_scalar(out=ve1[:], in0=ag1[:, 1:2], scalar1=EPS1, scalar2=None, op0=ALU.add)
                            y1d.inc(V.drain())
                            y1d.wait(G)
                            G.tensor_tensor(out=vi1[:], in0=ve1[:], in1=kmhalf[:], op=ALU.pow)
                            g1d.inc(G.drain())
                            g1d.wait(V)
                            if ti >= 2:
                                V.wait_ge(xprdy.h, ti - 1)
                                V.wait_ge(dh1w[ti % 2].h, 16 * (ti // 2))
                            V.tensor_scalar(out=hb1[k % 2][:], in0=xs[:], scalar1=ag1[:, 0:1],
                                            scalar2=vi1[:], op0=ALU.subtract, op1=ALU.mult)
                            V.drain()
                            vLN.inc(V.drain())
                            V.tensor_tensor(out=hb1[k % 2][:], in0=hb1[k % 2][:], in1=ca1[:], op=ALU.mult); V.drain()
                            V.tensor_tensor(out=hb1[k % 2][:], in0=hb1[k % 2][:], in1=sav[:], op=ALU.add)
                            vh1.inc(V.drain())
                            vh1.wait(Y)
                            bi = Y.dma_start(out=h1tok_d.ap()[128 * ti:128 * (ti + 1), :], in_=hb1[k % 2][:])
                            dh1w[ti % 2].inc(bi, 16)
                            vh1.wait(T)
                            xp = xpA if k % 2 == 0 else xpB
                            if ti >= 2:
                                T.wait_ge(h1wld.h, ti - 1)
                            for c in range(NCH):
                                T.transpose(xp[:, 128 * c:128 * (c + 1)], hb1[k % 2][:, 128 * c:128 * (c + 1)], ident[:])
                            xprdy.inc(T.drain())
                            xprdy.wait(C)
                            C.copy(out=h1wW[:].rearrange("p (c n) -> p c n", c=NCH)[:, :, 128 * k:128 * (k + 1)],
                                   in_=xp[:].rearrange("p (c n) -> p c n", c=NCH))
                            h1wld.inc(C.drain())

                        h1wld.wait(T)
                        for kind, wsrc, dst in (("K", wk, kt), ("Q", wq, qtF)):
                            for p in range(NPK):
                                pj = pj1 if p % 2 == 0 else pj2
                                prev = pj_hist.get(("KQ", p % 2))
                                if prev:
                                    T.wait_ge(pjev.h, prev)
                                for c in range(NCH):
                                    T.matmul(pj[:, 0:512],
                                             wsrc[:, 1024 * c + 128 * p:1024 * c + 128 * (p + 1)],
                                             h1wW[:, 512 * c:512 * (c + 1)],
                                             start=(c == 0), stop=(c == NCH - 1),
                                             skip_group_check=True)
                                pjdn.inc(T.drain())
                                pjdn.wait(C)
                                C.copy(out=dst[:, 2048 * p + 512 * w:2048 * p + 512 * (w + 1)], in_=pj[:, 0:512])
                                pjev.inc(C.drain())
                                pj_hist[("KQ", p % 2)] = pjev.n
                        for k in range(4):
                            ti = 4 * w + k
                            prev = pj_hist.get("V")
                            if prev:
                                T.wait_ge(pjev.h, prev)
                            for c in range(NCH):
                                for half in range(2):
                                    T.matmul(pjV[:, 512 * half:512 * (half + 1)],
                                             h1wW[:, 512 * c + 128 * k:512 * c + 128 * (k + 1)],
                                             wv[:, 1024 * c + 512 * half:1024 * c + 512 * (half + 1)],
                                             start=(c == 0), stop=(c == NCH - 1),
                                             skip_group_check=True)
                            pjdn.inc(T.drain())
                            pjdn.wait(C)
                            C.copy(out=vex[:].rearrange("p (s c) -> p s c", c=65)[:, H * ti:H * (ti + 1), 0:64],
                                   in_=pjV[:].rearrange("p (s c) -> p s c", c=64))
                            pjev.inc(C.drain())
                            pj_hist["V"] = pjev.n
                    p1dn.inc(T.drain())

                with contextlib.ExitStack() as p2_es:
                    sb2 = lambda n, sh, d=BF16: p2_es.enter_context(nc.sbuf_tensor(n, sh, d))
                    NATR = 16
                    atr = [sb2(f"atr{i}", [128, 1024]) for i in range(NATR)]
                    hTb = sb2("hTb", [128, 4 * D])
                    osb1 = sb2("osb1", [128, 512])
                    linv = [sb2(f"linvsb{i}", [128, 4], F32) for i in range(2)]
                    yt = sb2("ytsb", [128, D], F32)
                    z2 = sb2("z2sb", [128, D])
                    ab = sb2("absb", [128, D])
                    x1t = [sb2(f"x1t{i}", [128, D], F32) for i in range(2)]
                    x0r = [sb2(f"x0r{i}", [128, D], F32) for i in range(2)]
                    h1r = [sb2(f"h1r{i}", [128, D]) for i in range(2)]
                    gav = sb2("gavsb", [128, D]); gmt = sb2("gmtsb", [128, D]); btt = sb2("bttsb", [128, D])
                    qz = [sb2(f"qz{i}", [128, 512]) for i in range(2)]

                    sC = [p2_es.enter_context(nc.psum_tensor(f"sC{i}", [128, 1024], F32)) for i in range(2)]
                    oPd = [p2_es.enter_context(nc.psum_tensor(f"oPd{i}", [128, 512], F32)) for i in range(2)]
                    woP = p2_es.enter_context(nc.psum_tensor("woP", [128, 512], F32))

                    bi = G.dma_start(out=mp0a[:], in_=gam_d.ap()[:, :]); dgw.inc(bi, 16)
                    bi = G.dma_start(out=mp0b[:], in_=bet_d.ap()[:, :]); dgw.inc(bi, 16)
                    dgw.wait(T)
                    if True:
                        bp2 = woP
                        def bc2(dstt, srcrow):
                            for hh in range(2):
                                sD.wait(T)
                                T.matmul(bp2[:, 0:512], onesr[0:1, 0:128],
                                         srcrow[0:1, 512 * hh:512 * (hh + 1)], start=True, stop=True)
                                pC.inc(T.drain())
                                pC.wait(C)
                                C.copy(out=dstt[:, 512 * hh:512 * (hh + 1)], in_=bp2[:, 0:512])
                                sD.inc(C.drain())

                        bc2(gmt, mp0a)
                        bc2(btt, mp0b)
                        C.wait_ge(pC.h, pC.n)
                        bi = C.dma_start(out=mp0a[:], in_=mod[8:9, 2 * D:3 * D]); dscq.inc(bi, 16)
                        dscq.wait(T)
                        bc2(gav, mp0a)
                        sD.wait(V)

                    C.wait_ge(dh1w[0].h, 16 * NT // 2)
                    C.wait_ge(dh1w[1].h, 16 * NT // 2)
                    sc_hist = {}
                    atr_hist = {}
                    nrm_hist = {}
                    hT_hist = {}
                    qz_hist = {}
                    oxp_hist = {}
                    yadd = sm("yadd")
                    qzd = sm("qzd")
                    p1dn.wait(V)
                    qzm = sm("qzm")
                    V.memset(qz[0][:], 0.0); V.memset(qz[1][:], 0.0)
                    qzm.inc(V.drain())
                    qzm.wait(Y)
                    qzdd = [sm("qzdd0"), sm("qzdd1")]

                    def issue_qz(ghq):
                        g2, hq2 = divmod(ghq, 16)
                        e0q2 = 64 * (hq2 % 2)
                        pkq2 = hq2 // 2
                        prevq2 = qz_hist.get(hq2 % 2)
                        if prevq2:
                            Y.wait_ge(srdy.h, prevq2)
                        bi2 = Y.dma_start(
                            out=qz[hq2 % 2][e0q2:e0q2 + 64, :],
                            in_=qtF[e0q2:e0q2 + 64, 2048 * pkq2 + 512 * g2:2048 * pkq2 + 512 * (g2 + 1)])
                        qzdd[hq2 % 2].inc(bi2, 16)

                    issue_qz(0)
                    cbuf = 0
                    acnt = 0
                    for g in range(4):
                        qis = tuple(4 * g + j for j in range(4))
                        ntile = 4 * g + 4
                        nchunks = (ntile + 1) // 2
                        used_this_g = set()
                        for j in range(2):
                            qi = qis[j]
                            if g == 0 and j == 0:
                                p1dn.wait(Y)
                            if g >= 1:
                                C.wait_ge(vD.h, 4 * g)
                                Y.wait_ge(x1dn.h, 4 * (g - 1) + j + 3)
                            bi = C.dma_start(out=h1r[j][:], in_=h1tok_d.ap()[128 * qi:128 * (qi + 1), :])
                            dh1r[j].inc(bi, 16)
                            bi = Y.dma_start(out=x0r[j][:], in_=x0_d.ap()[128 * qi:128 * (qi + 1), :])
                            dx0r[j].inc(bi, 16)

                        chunk_rec = {}
                        av_first = {}
                        for hh in range(H + 2):
                            if hh < H:
                                recs = []
                                av_first[hh] = True
                                if 16 * g + hh + 1 < 64:
                                    issue_qz(16 * g + hh + 1)
                            for ci in range(nchunks):
                                if hh < H:
                                    h = hh
                                    pk = h // 2
                                    e0 = 64 * (h % 2)
                                    t_lo = 2 * ci
                                    t_hi = min(ntile, 2 * (ci + 1))
                                    buf = cbuf % 2
                                    aslot = acnt % NATR
                                    cbuf += 1; acnt += 1
                                    prev = sc_hist.get(buf)
                                    if prev:
                                        T.wait_ge(expd.h, prev)
                                    prev_av = atr_hist.get(aslot)
                                    if prev_av:
                                        C.wait_ge(avd.h, prev_av)
                                    if ci == 0:
                                        qzdd[h % 2].wait(T)
                                    for t2 in range(t_lo, t_hi):
                                        jj = t2 - t_lo
                                        T.matmul(sC[buf][:, 512 * jj:512 * (jj + 1)],
                                                 kt[:, 2048 * pk + 128 * t2:2048 * pk + 128 * (t2 + 1)],
                                                 qz[h % 2][:, :],
                                                 start=True, stop=True,
                                                 skip_group_check=True)
                                    srdy.inc(T.drain())
                                    srdy.wait(C)
                                    C.activation(out=atr[aslot][:, 0:512 * (t_hi - t_lo)],
                                                 in_=sC[buf][:, 0:512 * (t_hi - t_lo)],
                                                 func=AF.Exp, scale=0.125, bias=km20[:, 0:1])
                                    expd.inc(C.drain())
                                    sc_hist[buf] = expd.n
                                    diag = [t2 for t2 in range(t_lo, t_hi) if t2 in qis]
                                    mskn = None
                                    if diag:
                                        expd.wait(V)
                                        for t2 in diag:
                                            jj = t2 - t_lo
                                            k = t2 - 4 * g
                                            V.tensor_tensor(
                                                out=atr[aslot][:, 512 * jj + 128 * k:512 * jj + 128 * (k + 1)],
                                                in0=atr[aslot][:, 512 * jj + 128 * k:512 * jj + 128 * (k + 1)],
                                                in1=mask[:], op=ALU.mult)
                                        mskd.inc(V.drain())
                                        mskn = mskd.n
                                    recs.append((aslot, t_lo, t_hi, expd.n, diag, mskn))
                                if 1 <= hh <= H:
                                    h = hh - 1
                                    gh = 16 * g + h
                                    (aslot, t_lo, t_hi, expn, diag, mskn) = chunk_rec[h][ci]
                                    if av_first[h]:
                                        prevn = nrm_hist.get(gh - 2)
                                        if prevn:
                                            T.wait_ge(nrm.h, prevn)
                                    T.wait_ge(expd.h, expn)
                                    for j2, qi in enumerate(qis):
                                        for t2 in range(t_lo, min(t_hi, qi + 1)):
                                            if t2 == qi:
                                                continue
                                            jj = t2 - t_lo
                                            T.matmul(oPd[h % 2][:, 128 * j2:128 * j2 + 65],
                                                     atr[aslot][:, 512 * jj + 128 * j2:512 * jj + 128 * (j2 + 1)],
                                                     vex[:, 1040 * t2 + 65 * h:1040 * t2 + 65 * h + 65],
                                                     start=av_first[h], stop=False, skip_group_check=True)
                                            av_first[h] = False
                                    if diag:
                                        T.wait_ge(mskd.h, mskn)
                                        for t2 in diag:
                                            j2 = t2 - 4 * g
                                            jj = t2 - t_lo
                                            T.matmul(oPd[h % 2][:, 128 * j2:128 * j2 + 65],
                                                     atr[aslot][:, 512 * jj + 128 * j2:512 * jj + 128 * (j2 + 1)],
                                                     vex[:, 1040 * t2 + 65 * h:1040 * t2 + 65 * h + 65],
                                                     start=av_first[h], stop=False, skip_group_check=True)
                                            av_first[h] = False


                            if hh < H:
                                chunk_rec[hh] = recs
                                qz_hist[hh % 2] = srdy.n
                            if 1 <= hh <= H:
                                h = hh - 1
                                gh = 16 * g + h
                                avd.inc(T.drain())
                                for (aslot, _, _, _, _, _) in chunk_rec[h]:
                                    atr_hist[aslot] = avd.n
                                avd.wait(V)
                                ob0 = 256 * (h % 2)
                                for j2 in range(4):
                                    V.reciprocal(linv[h % 2][:, j2:j2 + 1], oPd[h % 2][:, 128 * j2 + 64:128 * j2 + 65])
                                V.drain()
                                for j2 in range(4):
                                    V.tensor_scalar(out=osb1[:, ob0 + 64 * j2:ob0 + 64 * (j2 + 1)],
                                                    in0=oPd[h % 2][:, 128 * j2:128 * j2 + 64],
                                                    scalar1=linv[h % 2][:, j2:j2 + 1], scalar2=None, op0=ALU.mult)
                                nrm.inc(V.drain())
                                nrm_hist[gh] = nrm.n
                            if 2 <= hh:
                                h = hh - 2
                                gh = 16 * g + h
                                pk = h // 2
                                e0 = 64 * (h % 2)
                                ob0 = 256 * (h % 2)
                                if h == 0 and g >= 1:
                                    V.wait_ge(wod.h, 8 * g)
                                obv = osb1[:, ob0:ob0 + 256].rearrange("p (j b c) -> p j b c", j=4, b=2)
                                for bi2 in range(4):
                                    for bj in range(2):
                                        otv = hTb[e0 + 32 * bj:e0 + 32 * bj + 32, :] \
                                            .rearrange("p (j x) -> p j x", j=4)[:, :, 128 * pk + 32 * bi2:128 * pk + 32 * bi2 + 32]
                                        V.transpose(out=otv,
                                                    in_=obv[32 * bi2:32 * bi2 + 32, :, bj, :])
                                hTd.inc(V.drain())
                                hT_hist[gh] = hTd.n

                        hTd.wait(T)
                        for j, qi in enumerate(qis):
                            for half in range(2):
                                if wod.n >= 1:
                                    T.wait_ge(yadd.h, wod.n)
                                for c in range(NCH):
                                    T.matmul(woP[:, 0:512],
                                             hTb[:, 1024 * j + 128 * c:1024 * j + 128 * (c + 1)],
                                             wosb[:, 1024 * c + 512 * half:1024 * c + 512 * (half + 1)],
                                             start=(c == 0), stop=(c == NCH - 1),
                                             skip_group_check=True)
                                wod.inc(T.drain())
                                wod.wait(V)
                                if half == 0:
                                    V.wait_ge(dh1r[j % 2].h, dh1r[j % 2].n)
                                V.tensor_tensor(out=yt[:, 512 * half:512 * (half + 1)], in0=woP[:, 0:512],
                                                in1=h1r[j % 2][:, 512 * half:512 * (half + 1)], op=ALU.add)
                                yadd.inc(V.drain())
                            vD.inc(V.drain())
                            V.bn_stats(st2[:, 0:6], yt[:, 0:512]); V.drain()
                            V.bn_stats(st2[:, 6:12], yt[:, 512:1024]); V.drain()
                            V.bn_aggr(ag2[:], st2[:]); V.drain()
                            V.tensor_scalar(out=ve2[:], in0=ag2[:, 1:2], scalar1=EPS2, scalar2=None, op0=ALU.add)
                            yLd.inc(V.drain())
                            yLd.wait(G)
                            G.tensor_tensor(out=vi2[:], in0=ve2[:], in1=kmhalf[:], op=ALU.pow)
                            gLd.inc(G.drain())
                            gLd.wait(V)
                            V.tensor_scalar(out=z2[:], in0=yt[:], scalar1=ag2[:, 0:1], scalar2=vi2[:],
                                            op0=ALU.subtract, op1=ALU.mult)
                            V.drain()
                            V.tensor_tensor(out=z2[:], in0=z2[:], in1=gmt[:], op=ALU.mult); V.drain()
                            V.tensor_tensor(out=ab[:], in0=z2[:], in1=btt[:], op=ALU.add); V.drain()
                            par = qi % 2
                            if qi >= 2:
                                V.wait_ge(dx1s[par].h, 16 * (qi // 2))
                            V.tensor_tensor(out=ab[:], in0=ab[:], in1=gav[:], op=ALU.mult); V.drain()
                            V.wait_ge(dx0r[j % 2].h, dx0r[j % 2].n)
                            V.tensor_tensor(out=x1t[par][:], in0=x0r[j % 2][:], in1=ab[:], op=ALU.add)
                            x1dn.inc(V.drain())
                            x1dn.wait(Y)
                            bi = Y.dma_start(
                                out=x1send_h[qi // 8].ap()[qi % 8, :, :],
                                in_=x1t[par][:])
                            dx1s[par].inc(bi, 16)
                            if j + 2 < 4:
                                qn = qis[j + 2]
                                C.wait_ge(vD.h, vD.n)
                                bi = C.dma_start(out=h1r[j % 2][:], in_=h1tok_d.ap()[128 * qn:128 * (qn + 1), :])
                                dh1r[j % 2].inc(bi, 16)
                                Y.wait_ge(x1dn.h, x1dn.n)
                                bi = Y.dma_start(out=x0r[j % 2][:], in_=x0_d.ap()[128 * qn:128 * (qn + 1), :])
                                dx0r[j % 2].inc(bi, 16)

                        if g == 1:
                            from concourse.aot_env import allow_ici_all_to_all
                            G.wait_ge(dx1s[0].h, 64); G.wait_ge(dx1s[1].h, 64)
                            with allow_ici_all_to_all():
                                bi = G.collective_compute(
                                    "AllToAll", ALU.bypass,
                                    replica_groups=[list(range(8))],
                                    ins=[x1send_h[0].ap()], outs=[x1recv_h[0].ap()],
                                )
                            ccs.inc(bi, 1)

                pE.inc(T.drain())
                if dbg:
                    dx1s[0].wait(G); dx1s[1].wait(G)
                    for hh in range(2):
                        bi = G.dma_start(
                            out=dbg_x1.ap()[1024 * hh:1024 * (hh + 1), :],
                            in_=x1send_h[hh].ap().rearrange("b r d -> (b r) d")[:, :])
                        dgw.inc(bi, 16)

            from concourse.aot_env import allow_ici_all_to_all
            dx1s[0].wait(G); dx1s[1].wait(G)
            with allow_ici_all_to_all():
                bi = G.collective_compute(
                    "AllToAll", ALU.bypass,
                    replica_groups=[list(range(8))],
                    ins=[x1send_h[1].ap()], outs=[x1recv_h[1].ap()],
                )
            ccs2.inc(bi, 1)

            h2rdy = sm("h2rdy"); xfd = sm("xfd"); hevc = sm("hevc")
            grd = sm("grd"); gevc = sm("gevc"); cupd = sm("cupd")
            bcd = sm("bcd"); bce = sm("bce"); zfd = sm("zfd")
            lnf = sm("lnf"); glnf = sm("glnf"); c0d = sm("c0d")

            with contextlib.ExitStack() as ffA:
                sbf = lambda n, sh, d=BF16: ffA.enter_context(nc.sbuf_tensor(n, sh, d))
                h2T = sbf("h2T", [128, B * 2048])
                cst = sbf("cst", [128, 2048])
                uoo = sbf("uoo", [128, 2048])
                gbt = [sbf(f"gbt{i}", [128, D]) for i in range(2)]

                with contextlib.ExitStack() as ffB:
                    sbb = lambda n, sh, d=BF16: ffB.enter_context(nc.sbuf_tensor(n, sh, d))
                    gwt = {}
                    for nm in ("wfT", "ufT", "wiT", "uiT", "wcT", "ucT"):
                        gwt[nm] = sbb(nm + "sb", [128, 8192])
                    bp0 = [sbb(f"bp0{i}", [1, D]) for i in range(3)]
                    cb1 = [sbb(f"cb1{i}", [128, D]) for i in range(2)]
                    sbt = [sbb(f"sbt{i}", [128, D]) for i in range(2)]
                    fsb = sbb("fsb", [128, 2048])
                    isb = sbb("isb", [128, 2048])
                    gsb = sbb("gsb", [128, 2048])
                    x1s = [sbb(f"x1s{i}", [128, D], F32) for i in range(2)]
                    zf = sbb("zf", [128, D])
                    h2tok = [sbb(f"h2tok{i}", [128, D]) for i in range(2)]

                    pE.wait(G)
                    G.wait_ge(x1dn.h, x1dn.n)
                    G.wait_ge(hTd.h, hTd.n)
                    dgr = sm("dgr")
                    for i, r in enumerate((0, 1, 2)):
                        bi = G.dma_start(out=bp0[i][:], in_=brow_d.ap()[r:r + 1, :])
                        dgr.inc(bi, 16)
                    C.wait_ge(pC.h, pC.n)
                    bi = G.dma_start(out=mp0a[:], in_=brow_d.ap()[4:5, :])
                    dgr.inc(bi, 16)
                    dgf3 = [sm(f"dgf3{i}") for i in range(3)]
                    for gi2, pair in enumerate((("wfT", wfT_d, "ufT", ufT_d),
                                                ("wiT", wiT_d, "uiT", uiT_d),
                                                ("wcT", wcT_d, "ucT", ucT_d))):
                        for nm, srcd in (pair[0:2], pair[2:4]):
                            bi = G.dma_start(out=gwt[nm][:].rearrange("p (c n) -> p c n", c=8),
                                             in_=srcd.ap().rearrange("(c p) n -> p c n", p=128))
                            dgf3[gi2].inc(bi, 16)

                    xpf = [ffB.enter_context(nc.psum_tensor(f"xpf{i}", [128, 1024], BF16)) for i in range(2)]
                    bpf = ffB.enter_context(nc.psum_tensor("bpf", [128, 512], F32))
                    rg = ffB.enter_context(nc.psum_tensor("Rg", [128, 2048], F32))

                    def ff_bcast(dstt, srcrow, guard=None):
                        for hh in range(2):
                            bce.wait(T)
                            T.matmul(bpf[:, 0:512], onesr[0:1, 0:128],
                                     srcrow[0:1, 512 * hh:512 * (hh + 1)], start=True, stop=True)
                            bcd.inc(T.drain())
                            bcd.wait(C)
                            if guard is not None:
                                C.wait_ge(guard[0].h, guard[1])
                            C.copy(out=dstt[:, 512 * hh:512 * (hh + 1)], in_=bpf[:, 0:512])
                            bce.inc(C.drain())

                    dgr.wait(T)
                    for mi in range(NCH):
                        bce.wait(T)
                        T.matmul(bpf[:, 0:256], mp0a[0:1, 128 * mi:128 * (mi + 1)], onesr[0:1, 0:256],
                                 start=True, stop=True)
                        c0d.inc(T.drain())
                        c0d.wait(C)
                        C.copy(out=cst[:, 256 * mi:256 * (mi + 1)], in_=bpf[:, 0:256])
                        bce.inc(C.drain())

                    ccs.wait(Y)
                    c0d.wait(C)
                    def prep_bcast(t):
                        tp2 = t % 2
                        if t >= 2:
                            C.wait_ge(h2rdy.h, 2 * t - 2)
                        C.wait_ge(bcd.h, bcd.n)
                        bi2 = C.dma_start(out=mp0a[:], in_=mod[t:t + 1, 4 * D:5 * D]); dscq.inc(bi2, 16)
                        bi2 = C.dma_start(out=mp0b[:], in_=mod[t:t + 1, 3 * D:4 * D]); dscq.inc(bi2, 16)
                        dscq.wait(T)
                        ff_bcast(cb1[tp2], mp0a)
                        ff_bcast(sbt[tp2], mp0b)
                    def prep_ln(t, rt):
                        tp = t % 2
                        it = 2 * t + rt
                        sl = it % 2
                        if it >= 2:
                            Y.wait_ge(zfd.h, it - 1)
                        if rt == 1 and t == 0:
                            ccs2.wait(Y)
                        bi = Y.dma_start(out=x1s[sl][:], in_=x1recv_h[rt].ap()[t, :, :])
                        dxr[sl].inc(bi, 16)
                        V.wait_ge(dxr[sl].h, dxr[sl].n)
                        V.bn_stats(st1[:, 0:6], x1s[sl][:, 0:512]); V.drain()
                        V.bn_stats(st1[:, 6:12], x1s[sl][:, 512:1024]); V.drain()
                        V.bn_aggr(ag1[:], st1[:]); V.drain()
                        V.tensor_scalar(out=ve1[:], in0=ag1[:, 1:2], scalar1=EPS1, scalar2=None, op0=ALU.add)
                        lnf.inc(V.drain())
                        lnf.wait(G)
                        G.tensor_tensor(out=vi1[:], in0=ve1[:], in1=kmhalf[:], op=ALU.pow)
                        glnf.inc(G.drain())
                        glnf.wait(V)
                        V.tensor_scalar(out=zf[:], in0=x1s[sl][:], scalar1=ag1[:, 0:1], scalar2=vi1[:],
                                        op0=ALU.subtract, op1=ALU.mult)
                        zfd.inc(V.drain())
                        V.wait_ge(bce.h, 8 + 4 * (t + 1))
                        if it >= 2:
                            V.wait_ge(xfd.h, it - 1)
                        V.tensor_tensor(out=h2tok[sl][:], in0=zf[:], in1=cb1[tp][:], op=ALU.mult); V.drain()
                        V.tensor_tensor(out=h2tok[sl][:], in0=h2tok[sl][:], in1=sbt[tp][:], op=ALU.add)
                        h2rdy.inc(V.drain())

                    def prep_tr(t, rt):
                        it = 2 * t + rt
                        sl = it % 2
                        T.wait_ge(h2rdy.h, it + 1)
                        if it >= 2:
                            T.wait_ge(hevc.h, it - 1)
                        for c in range(NCH):
                            T.transpose(xpf[sl][:, 128 * c:128 * (c + 1)], h2tok[sl][:, 128 * c:128 * (c + 1)], ident[:])
                        xfd.inc(T.drain())
                        xfd.wait(C)
                        C.copy(out=h2T[:].rearrange("p (t c r) -> p t c r", t=B, c=NCH)
                               [:, t, :, 128 * rt:128 * (rt + 1)],
                               in_=xpf[sl][:].rearrange("p (c r) -> p c r", c=NCH))
                        hevc.inc(C.drain())

                    gates = (("wfT", "ufT", 0, AF.Sigmoid, fsb), ("wiT", "uiT", 1, AF.Sigmoid, isb),
                             ("wcT", "ucT", 2, AF.Tanh, gsb))

                    def gate_step(t):
                        for gi, (wn, un, bidx, fn, dstt) in enumerate(gates):
                            if t == 0:
                                T.wait_ge(dgf3[gi].h, 32)
                                if gi == 0:
                                    T.wait_ge(bce.h, 8)
                            gevc.wait(T)
                            T.wait_ge(hevc.h, 2 * (t + 1))
                            T.wait_ge(cupd.h, t)
                            for bank in range(4):
                                for k in range(2):
                                    mi = 2 * bank + k
                                    T.matmul(rg[:, 256 * mi:256 * (mi + 1)],
                                             bp0[bidx][0:1, 128 * mi:128 * (mi + 1)], onesr[0:1, 0:256],
                                             start=(k == 0), stop=False, skip_group_check=True)
                                for k in range(2):
                                    mi = 2 * bank + k
                                    for c in range(NCH):
                                        T.matmul(rg[:, 256 * mi:256 * (mi + 1)],
                                                 gwt[wn][:, 1024 * c + 128 * mi:1024 * c + 128 * (mi + 1)],
                                                 h2T[:, 2048 * t + 256 * c:2048 * t + 256 * (c + 1)],
                                                 start=False, stop=False, skip_group_check=True)
                                for k in range(2):
                                    mi = 2 * bank + k
                                    for c in range(NCH):
                                        T.matmul(rg[:, 256 * mi:256 * (mi + 1)],
                                                 gwt[un][:, 1024 * c + 128 * mi:1024 * c + 128 * (mi + 1)],
                                                 cst[:, 256 * c:256 * (c + 1)],
                                                 start=False, stop=(k == 1 and c == NCH - 1),
                                                 skip_group_check=True)
                            grd.inc(T.drain())
                            grd.wait(C)
                            C.activation(out=dstt[:], in_=rg[:, 0:2048], func=fn)
                            gevc.inc(C.drain())
                        gevc.wait(V)
                        V.tensor_tensor(out=fsb[:], in0=fsb[:], in1=cst[:], op=ALU.mult); V.drain()
                        V.tensor_tensor(out=isb[:], in0=isb[:], in1=gsb[:], op=ALU.mult); V.drain()
                        V.tensor_tensor(out=cst[:], in0=fsb[:], in1=isb[:], op=ALU.add)
                        cupd.inc(V.drain())

                    prep_bcast(0); prep_bcast(1)
                    prep_ln(0, 0); prep_ln(0, 1)
                    prep_tr(0, 0); prep_tr(0, 1)
                    prep_bcast(2)
                    prep_ln(1, 0); prep_ln(1, 1)
                    for t in range(B):
                        gate_step(t)
                        if t + 1 < B:
                            prep_tr(t + 1, 0); prep_tr(t + 1, 1)
                        if t + 3 < B:
                            prep_bcast(t + 3)
                        if t + 2 < B:
                            prep_ln(t + 2, 0); prep_ln(t + 2, 1)
                    if dbg:
                        G.wait_ge(hevc.h, hevc.n)
                        bi = G.dma_start(out=dbg_h2.ap()[:, :], in_=h2T[:, :])
                        dgf.inc(bi, 16)
                    if dbg:
                        G.wait_ge(cupd.h, cupd.n)
                        bi = G.dma_start(out=dbg_c.ap()[:, :], in_=cst[:, :])
                        dgf.inc(bi, 16)

                with contextlib.ExitStack() as ffC:
                    sbc = lambda n, sh, d=BF16: ffC.enter_context(nc.sbuf_tensor(n, sh, d))
                    woTs = sbc("woTs", [128, 8192])
                    uoTs = sbc("uoTs", [128, 8192])
                    bp0o = sbc("bp0o", [1, D])
                    gbt8 = [sbc(f"gbt8{i}", [128, D]) for i in range(B)]
                    x1f = [sbc(f"x1f{i}", [128, D], F32) for i in range(2)]
                    tmpf = [sbc(f"tmpf{i}", [128, D], F32) for i in range(2)]
                    hb = [sbc(f"hb{i}", [128, D]) for i in range(2)]
                    outt = [sbc(f"outt{i}", [128, D], F32) for i in range(2)]
                    wps = [ffC.enter_context(nc.psum_tensor(f"wps{i}", [128, 1024], F32)) for i in range(2)]
                    bpo = ffC.enter_context(nc.psum_tensor("bpo", [128, 512], F32))

                    uod = sm("uod"); uoe = sm("uoe"); wpd = sm("wpd")
                    fvd = sm("fvd"); ftd = sm("ftd"); odn = sm("odn")
                    fbd = sm("fbd"); fbe = sm("fbe")

                    G.wait_ge(cupd.h, cupd.n)
                    G.wait_ge(gevc.h, gevc.n)
                    dgu = sm("dgu")
                    bi = G.dma_start(out=bp0o[:], in_=brow_d.ap()[3:4, :])
                    dgu.inc(bi, 16)
                    bi = G.dma_start(out=uoTs[:].rearrange("p (c n) -> p c n", c=8),
                                     in_=uoT_d.ap().rearrange("(c p) n -> p c n", p=128))
                    dgu.inc(bi, 16)
                    bi = G.dma_start(out=woTs[:].rearrange("p (c n) -> p c n", c=8),
                                     in_=woT_d.ap().rearrange("(c p) n -> p c n", p=128))
                    dgf.inc(bi, 16)

                    def fin_bcast(dstt, srcrow):
                        for hh in range(2):
                            fbe.wait(T)
                            T.matmul(bpo[:, 0:512], onesr[0:1, 0:128],
                                     srcrow[0:1, 512 * hh:512 * (hh + 1)], start=True, stop=True)
                            fbd.inc(T.drain())
                            fbd.wait(C)
                            C.copy(out=dstt[:, 512 * hh:512 * (hh + 1)], in_=bpo[:, 0:512])
                            fbe.inc(C.drain())

                    dgu.wait(T)
                    cupd.wait(T)
                    cupd.wait(Y)
                    for rt in range(2):
                        for half in range(2):
                            T.matmul(wps[rt][:, 512 * half:512 * (half + 1)],
                                     onesr[0:1, 0:128], bp0o[0:1, 512 * half:512 * (half + 1)],
                                     start=True, stop=False, skip_group_check=True)
                            for c in range(NCH):
                                T.matmul(wps[rt][:, 512 * half:512 * (half + 1)],
                                         cst[:, 256 * c + 128 * rt:256 * c + 128 * (rt + 1)],
                                         uoTs[:, 1024 * c + 512 * half:1024 * c + 512 * (half + 1)],
                                         start=False, stop=(c == NCH - 1), skip_group_check=True)
                        uod.inc(T.drain())
                        uod.wait(C)
                        C.copy(out=uoo[:, 1024 * rt:1024 * (rt + 1)], in_=wps[rt][:, 0:1024])
                        uoe.inc(C.drain())

                    def gb_build(t):
                        C.wait_ge(fbd.h, fbd.n)
                        bi = C.dma_start(out=mp0b[:], in_=mod[t:t + 1, 5 * D:6 * D]); dscq.inc(bi, 16)
                        dscq.wait(T)
                        fin_bcast(gbt8[t], mp0b)

                    gb_build(0); gb_build(1)
                    for t in range(B):
                        for rt in range(2):
                            it = 2 * t + rt
                            sl = it % 2
                            T.wait_ge(uoe.h, 2)
                            if it == 0:
                                dgf.wait(T)
                            if it >= 2:
                                T.wait_ge(fvd.h, it - 1)
                            for half in range(2):
                                for c in range(NCH):
                                    T.matmul(wps[sl][:, 512 * half:512 * (half + 1)],
                                             h2T[:, 2048 * t + 256 * c + 128 * rt:2048 * t + 256 * c + 128 * (rt + 1)],
                                             woTs[:, 1024 * c + 512 * half:1024 * c + 512 * (half + 1)],
                                             start=(c == 0), stop=(c == NCH - 1), skip_group_check=True)
                            wpd.inc(T.drain())
                            if rt == 1 and t + 2 < B:
                                gb_build(t + 2)
                            if it >= 2:
                                Y.wait_ge(odn.h, it - 1)
                            bi = Y.dma_start(out=x1f[sl][:], in_=x1recv_h[rt].ap()[t, :, :])
                            dxf.inc(bi, 16)
                            wpd.wait(V)
                            V.wait_ge(uoe.h, 2)
                            if it >= 2:
                                V.wait_ge(ftd.h, it - 1)
                            V.tensor_tensor(out=tmpf[sl][:], in0=wps[sl][:, 0:1024],
                                            in1=uoo[:, 1024 * rt:1024 * (rt + 1)], op=ALU.add)
                            fvd.inc(V.drain())
                            fvd.wait(C)
                            C.activation(out=hb[sl][:], in_=tmpf[sl][:], func=AF.Tanh)
                            ftd.inc(C.drain())
                            ftd.wait(V)
                            V.wait_ge(fbe.h, 2 * (t + 1))
                            V.tensor_tensor(out=hb[sl][:], in0=hb[sl][:], in1=gbt8[t][:], op=ALU.mult); V.drain()
                            V.wait_ge(dxf.h, 16 * (it + 1))
                            if it >= 2:
                                V.wait_ge(dout[sl].h, 16 * (it // 2))
                            V.tensor_tensor(out=outt[sl][:], in0=x1f[sl][:], in1=hb[sl][:], op=ALU.add)
                            odn.inc(V.drain())
                            odn.wait(Y)
                            bi = Y.dma_start(out=out_d.ap()[t, 128 * rt:128 * (rt + 1), :], in_=outt[sl][:])
                            dout[sl].inc(bi, 16)

                    Y.wait_ge(dout[0].h, dout[0].n)
                    Y.wait_ge(dout[1].h, dout[1].n)
                    if dbg:
                        G.wait_ge(dgf.h, dgf.n)
    return nc


_NC_CACHE = {}


def _get_nc(dbg=False):
    key = bool(dbg)
    if key not in _NC_CACHE:
        _NC_CACHE[key] = build_nc(dbg=key)
    return _NC_CACHE[key]


def make_in_maps(inputs):
    x = np.ascontiguousarray(np.asarray(inputs["L0_attn_x"]), dtype=np.float32)
    c = np.ascontiguousarray(np.asarray(inputs["L0_c"]), dtype=np.float32)
    wmod = np.ascontiguousarray(np.asarray(inputs["L0_wmod"]), dtype=np.float32)
    wq = np.ascontiguousarray(np.asarray(inputs["L0_attn_Wq"]).transpose(1, 0, 2).reshape(D, D), dtype=np.float32)
    wk = np.ascontiguousarray(np.asarray(inputs["L0_attn_Wk"]).transpose(1, 0, 2).reshape(D, D), dtype=np.float32)
    wv = np.ascontiguousarray(np.asarray(inputs["L0_attn_Wv"]).transpose(1, 0, 2).reshape(D, D), dtype=np.float32)
    wo = np.ascontiguousarray(np.asarray(inputs["L0_attn_Wo"]), dtype=np.float32)
    gam = np.ascontiguousarray(np.asarray(inputs["L0_attn_gamma"]), dtype=np.float32).reshape(1, D)
    bet = np.ascontiguousarray(np.asarray(inputs["L0_attn_beta"]), dtype=np.float32).reshape(1, D)
    trs = lambda k: np.ascontiguousarray(np.asarray(inputs[k], dtype=np.float32).T)
    wfT, ufT = trs("L0_ffn_wf"), trs("L0_ffn_uf")
    wiT, uiT = trs("L0_ffn_wi"), trs("L0_ffn_ui")
    wcT, ucT = trs("L0_ffn_wc"), trs("L0_ffn_uc")
    woT, uoT = trs("L0_ffn_wo"), trs("L0_ffn_uo")
    brows = np.ascontiguousarray(np.stack([
        np.asarray(inputs["L0_ffn_bf"], dtype=np.float32),
        np.asarray(inputs["L0_ffn_bi"], dtype=np.float32),
        np.asarray(inputs["L0_ffn_bc"], dtype=np.float32),
        np.asarray(inputs["L0_ffn_bo"], dtype=np.float32),
        np.asarray(inputs["L0_ffn_in_c"], dtype=np.float32),
    ]))
    ident = np.eye(128, dtype=BF)
    masktile = np.triu(np.ones((128, 128), dtype=np.float32)).astype(BF)
    onesrow = np.ones((1, 512), dtype=BF)
    shared = dict(wmod=wmod, wq=wq, wk=wk, wv=wv, wo=wo, gamma=gam, beta=bet,
                  wfT=wfT, ufT=ufT, wiT=wiT, uiT=uiT, wcT=wcT, ucT=ucT,
                  woT=woT, uoT=uoT, brows=brows, ident=ident,
                  masktile=masktile, onesrow=onesrow)
    in_maps = []
    for b in range(B):
        m = dict(shared)
        m["x0"] = np.ascontiguousarray(x[b])
        m["c9"] = np.ascontiguousarray(np.concatenate([c, c[b:b + 1]], axis=0))
        in_maps.append(m)
    return in_maps


def kernel(**inputs):
    nc = _get_nc(dbg=False)
    in_maps = make_in_maps(inputs)
    res = run_bass_kernel_spmd(nc, in_maps, core_ids=list(range(8)))
    full = np.empty((B, S, D), dtype=np.float32)
    for j in range(8):
        o = np.asarray(res.results[j]["out"])
        full[:, 128 * j:128 * (j + 1), :] = o[:, 0:128, :]
        full[:, 1024 + 128 * j:1024 + 128 * (j + 1), :] = o[:, 128:256, :]
    return full


# version 27
# speedup vs baseline: 1.0841x; 1.0374x over previous
import os
os.environ.setdefault("TRN_ALLOW_ICI_ALL_TO_ALL", "1")

import contextlib
import numpy as np
import ml_dtypes

import concourse.bass as bass
import concourse.mybir as mybir
from concourse.bass_utils import run_bass_kernel_spmd

F32 = mybir.dt.float32
BF16 = mybir.dt.bfloat16
AF = mybir.ActivationFunctionType
ALU = mybir.AluOpType
BF = ml_dtypes.bfloat16

D = 1024
S = 2048
B = 8
H = 16
E = 64
NT = 16
NCH = 8
NPK = 8
R = 256
EPS1 = 1e-5
EPS2 = 1e-3
EXPB = -20.0


class Sem:
    def __init__(self, nc, es, name):
        self.h = es.enter_context(nc.semaphore(name))
        self.n = 0

    def inc(self, bi, k=1):
        bi.then_inc(self.h, k)
        self.n += k

    def wait(self, eng, n=None):
        v = self.n if n is None else n
        if v > 0:
            eng.wait_ge(self.h, v)


def build_nc(dbg=False):
    nc = bass.Bass("TRN2", target_bir_lowering=False)

    dt = lambda n, sh, k="ExternalInput", d=F32: nc.dram_tensor(n, sh, d, kind=k)
    x0_d = dt("x0", [S, D])
    c9_d = dt("c9", [9, D])
    wmod_d = dt("wmod", [D, 6 * D])
    wq_d = dt("wq", [D, D]); wk_d = dt("wk", [D, D]); wv_d = dt("wv", [D, D])
    wo_d = dt("wo", [D, D])
    gam_d = dt("gamma", [1, D]); bet_d = dt("beta", [1, D])
    wfT_d = dt("wfT", [D, D]); ufT_d = dt("ufT", [D, D])
    wiT_d = dt("wiT", [D, D]); uiT_d = dt("uiT", [D, D])
    wcT_d = dt("wcT", [D, D]); ucT_d = dt("ucT", [D, D])
    woT_d = dt("woT", [D, D]); uoT_d = dt("uoT", [D, D])
    brow_d = dt("brows", [5, D])
    ident_d = dt("ident", [128, 128], d=BF16)
    mask_d = dt("masktile", [128, 128], d=BF16)
    ones_d = dt("onesrow", [1, 512], d=BF16)
    out_d = dt("out", [B, R, D], k="ExternalOutput")
    x0f_d = dt("x0f", [B, R, D])
    x1send_h = [nc.dram_tensor("x1sendA", [B, 128, D], BF16),
                nc.dram_tensor("x1sendB", [B, 128, D], BF16)]
    h1tok_d = nc.dram_tensor("h1tok", [S, D], BF16)
    x1recv_h = [nc.dram_tensor("x1recvA", [B, 128, D], BF16),
                nc.dram_tensor("x1recvB", [B, 128, D], BF16)]
    if dbg:
        dbg_mod = nc.dram_tensor("dbg_mod", [9, 6 * D], F32, kind="ExternalOutput")
        dbg_x1 = nc.dram_tensor("dbg_x1", [S, D], F32, kind="ExternalOutput")
        dbg_c = nc.dram_tensor("dbg_c", [128, 2048], F32, kind="ExternalOutput")
        dbg_h2 = nc.dram_tensor("dbg_h2", [128, B * 2048], F32, kind="ExternalOutput")

    es = contextlib.ExitStack()
    with es:
        sb = lambda n, sh, d=BF16: es.enter_context(nc.sbuf_tensor(n, sh, d))
        ident = sb("identsb", [128, 128])
        mask = sb("masksb", [128, 128])
        onesr = sb("onesrsb", [1, 512])
        km20 = sb("km20", [128, 1], F32)
        kmhalf = sb("kmhalf", [128, 1], F32)
        mod = sb("modsb", [9, 6 * D])
        mp0a = sb("mp0a", [1, D])
        mp0b = sb("mp0b", [1, D])
        st1 = sb("st1", [128, 12], F32); ag1 = sb("ag1", [128, 2], F32)
        ve1 = sb("ve1", [128, 1], F32); vi1 = sb("vi1", [128, 1], F32)
        st2 = sb("st2", [128, 12], F32); ag2 = sb("ag2", [128, 2], F32)
        ve2 = sb("ve2", [128, 1], F32); vi2 = sb("vi2", [128, 1], F32)

        sm = lambda n: Sem(nc, es, n)
        dcon = sm("dcon"); dxr = [sm("dxr0"), sm("dxr1")]; dxf = sm("dxf"); dgwm = sm("dgwm")
        dx0s = [sm(f"dx0s{i}") for i in range(3)]
        dx1s = [sm(f"dx1s{i}") for i in range(2)]
        dout = [sm("dout0"), sm("dout1")]; dscq = sm("dscq"); dgw = sm("dgw"); dgf = sm("dgf")
        ccs = sm("ccs"); ccs2 = sm("ccs2")
        kons = sm("kons")

        T = nc.tensor; V = nc.vector; C = nc.scalar; G = nc.gpsimd; Y = nc.sync

        blk = es.enter_context(nc.Block())

        @blk.sync
        def _(sync):
            V.memset(km20[:], EXPB)
            V.memset(kmhalf[:], -0.5)
            kons.inc(V.drain())
            kons.wait(C)
            kons.wait(G)
            bi = Y.dma_start(out=ident[:], in_=ident_d.ap()[:, :]); dcon.inc(bi, 16)
            bi = Y.dma_start(out=mask[:], in_=mask_d.ap()[:, :]); dcon.inc(bi, 16)
            bi = Y.dma_start(out=onesr[:], in_=ones_d.ap()[:, :]); dcon.inc(bi, 16)

            svA = sm("svA"); spA = sm("spA"); spB = sm("spB")
            ssA = sm("ssA"); ssB = sm("ssB"); ssC = sm("ssC")

            with contextlib.ExitStack() as st_es:
                sbl = lambda n, sh, d=BF16: st_es.enter_context(nc.sbuf_tensor(n, sh, d))
                c9f = sbl("c9f", [9, D], F32)
                siluc = sbl("siluc", [9, D])
                silucT = sbl("silucT", [128, 9 * NCH])
                ghelp = sbl("ghelp", [9, 2 * D])
                cblk = sbl("cblk", [128, 8 * 6144])
                xps = st_es.enter_context(nc.psum_tensor("xps", [128, 128], BF16))
                mps = [st_es.enter_context(nc.psum_tensor(f"mps{i}", [9, 512], F32)) for i in range(2)]

                bi = Y.dma_start(out=c9f[:], in_=c9_d.ap()[:, :]); dcon.inc(bi, 16)
                wmv = wmod_d.ap().rearrange("(c p) n -> p c n", p=128)
                dgwm4 = [sm(f"dgwm4{i}") for i in range(4)]
                for s4 in range(4):
                    bi = G.dma_start(
                        out=cblk[:].rearrange("p (c n) -> p c n", c=8)[:, :, 1536 * s4:1536 * (s4 + 1)],
                        in_=wmv[:, :, 1536 * s4:1536 * (s4 + 1)])
                    dgwm4[s4].inc(bi, 16)

                dcon.wait(C)
                C.activation(out=siluc[:], in_=c9f[:], func=AF.Silu)
                ssA.inc(C.drain())
                ssA.wait(T)
                for c in range(NCH):
                    T.transpose(xps[0:128, 16 * c:16 * c + 9], siluc[0:9, 128 * c:128 * (c + 1)], ident[0:9, 0:9])
                spA.inc(T.drain())
                spA.wait(C)
                C.copy(out=silucT[:].rearrange("p (c n) -> p c n", c=NCH),
                       in_=xps[:].rearrange("p (c n) -> p c n", c=NCH)[:, :, 0:9])
                ssB.inc(C.drain())
                ssB.wait(T)
                for j in range(12):
                    T.wait_ge(dgwm4[j // 3].h, 16)
                    if j >= 2:
                        T.wait_ge(ssC.h, j - 1)
                    mp = mps[j % 2]
                    for c in range(NCH):
                        T.matmul(mp[0:9, 0:512], silucT[:, 9 * c:9 * (c + 1)],
                                 cblk[:, 6144 * c + 512 * j:6144 * c + 512 * (j + 1)],
                                 start=(c == 0), stop=(c == NCH - 1))
                    spB.inc(T.drain())
                    spB.wait(C)
                    C.copy(out=mod[0:9, 512 * j:512 * (j + 1)], in_=mp[0:9, 0:512])
                    ssC.inc(C.drain())
                ssC.wait(V)
                for lo in (D, 4 * D):
                    V.tensor_scalar(out=mod[0:9, lo:lo + D], in0=mod[0:9, lo:lo + D],
                                    scalar1=1.0, scalar2=None, op0=ALU.add)
                    V.drain()
                for lo in (2 * D, 5 * D):
                    sl = mod[0:9, lo:lo + D]
                    V.tensor_scalar(out=ghelp[0:9, 0:D], in0=sl, scalar1=0.0, scalar2=2.0,
                                    op0=ALU.is_ge, op1=ALU.mult)
                    V.drain()
                    V.tensor_scalar(out=ghelp[0:9, 0:D], in0=ghelp[0:9, 0:D], scalar1=-1.0,
                                    scalar2=None, op0=ALU.add)
                    V.drain()
                    V.tensor_scalar(out=ghelp[0:9, D:2 * D], in0=sl, scalar1=-1.0, scalar2=None,
                                    op0=ALU.mult)
                    V.drain()
                    V.tensor_tensor(out=ghelp[0:9, D:2 * D], in0=sl, in1=ghelp[0:9, D:2 * D], op=ALU.max)
                    V.drain()
                    V.tensor_scalar(out=ghelp[0:9, D:2 * D], in0=ghelp[0:9, D:2 * D], scalar1=0.25,
                                    scalar2=None, op0=ALU.max)
                    V.drain()
                    V.tensor_tensor(out=sl, in0=ghelp[0:9, 0:D], in1=ghelp[0:9, D:2 * D], op=ALU.mult)
                    V.drain()
                svA.inc(V.drain())
                if dbg:
                    svA.wait(G)
                    bi = G.dma_start(out=dbg_mod.ap()[:, :], in_=mod[0:9, :])
                    dgw.inc(bi, 16)

            pE = sm("pE"); x1dn = sm("x1dn"); hTd = sm("hTd")
            vh1 = sm("vh1"); xprdy = sm("xprdy"); h1wld = sm("h1wld")
            pjev = sm("pjev"); pjdn = sm("pjdn"); p1dn = sm("p1dn")
            dh1w = [sm("dh1w0"), sm("dh1w1")]; dh1r = [sm("dh1r0"), sm("dh1r1")]
            y1d = sm("y1d"); g1d = sm("g1d"); yLd = sm("yLd"); gLd = sm("gLd")
            vLN = sm("vLN")
            srdy = sm("srdy"); expd = sm("expd"); mskd = sm("mskd")
            avd = sm("avd"); nrm = sm("nrm"); oxp = sm("oxp")
            wod = sm("wod"); vD = sm("vD")
            pC = sm("pC"); sD = sm("sD"); vB = sm("vB")
            dx0r = [sm(f"dx0r{i}") for i in range(2)]

            with contextlib.ExitStack() as ao_es:
                sbo = lambda n, sh, d=BF16: ao_es.enter_context(nc.sbuf_tensor(n, sh, d))
                kt = sbo("ktsb", [128, NPK * S])
                vex = sbo("vexsb", [128, NT * (H * 65)])
                qtF = sbo("qtFsb", [128, NPK * S])
                wosb = ao_es.enter_context(nc.sbuf_tensor("wosb", [128, 8192], BF16, side="right"))

                with contextlib.ExitStack() as p1_es:
                    sb1 = lambda n, sh, d=BF16: p1_es.enter_context(nc.sbuf_tensor(n, sh, d))
                    sb1r = lambda n, sh, d=BF16: p1_es.enter_context(nc.sbuf_tensor(n, sh, d, side="right"))
                    wq = sb1r("wqsb", [128, 8192]); wk = sb1r("wksb", [128, 8192]); wv = sb1r("wvsb", [128, 8192])
                    h1wW = sb1("h1wW", [128, 8 * 512])
                    x0s = [sb1(f"x0s{i}", [128, D], F32) for i in range(3)]
                    hb1 = [sb1(f"hb1{i}", [128, D]) for i in range(2)]
                    ca1 = sb1("ca1sb", [128, D]); sav = sb1("savsb", [128, D])

                    for (dst, srcd) in ((wq, wq_d), (wk, wk_d), (wv, wv_d), (wosb, wo_d)):
                        bi = G.dma_start(out=dst[:].rearrange("p (c n) -> p c n", c=8),
                                         in_=srcd.ap().rearrange("(c p) n -> p c n", p=128))
                        dgw.inc(bi, 16)
                    V.wait_ge(spB.h, 12)
                    V.memset(vex[:].rearrange("p (s c) -> p s c", c=65)[:, :, 64:65], 1.0)
                    vB.inc(V.drain())

                    xpA = p1_es.enter_context(nc.psum_tensor("xpA", [128, 1024], BF16))
                    xpB = p1_es.enter_context(nc.psum_tensor("xpB", [128, 1024], BF16))
                    pj1 = p1_es.enter_context(nc.psum_tensor("pj1", [128, 512], F32))
                    pj2 = p1_es.enter_context(nc.psum_tensor("pj2", [128, 512], F32))
                    pjV = p1_es.enter_context(nc.psum_tensor("pjV", [128, 1024], F32))

                    dgw.wait(T)
                    dcon.wait(T)
                    svA.wait(C)
                    for dstt, lo in ((sav, 0), (ca1, D)):
                        C.wait_ge(pC.h, pC.n)
                        bi = C.dma_start(out=mp0a[:], in_=mod[8:9, lo:lo + D]); dscq.inc(bi, 16)
                        dscq.wait(T)
                        for hh in range(2):
                            sD.wait(T)
                            T.matmul(pj1[:, 0:512], onesr[0:1, 0:128],
                                     mp0a[0:1, 512 * hh:512 * (hh + 1)], start=True, stop=True)
                            pC.inc(T.drain())
                            pC.wait(C)
                            C.copy(out=dstt[:, 512 * hh:512 * (hh + 1)], in_=pj1[:, 0:512])
                            sD.inc(C.drain())

                    Y.wait_ge(spB.h, 12)
                    for k in range(2):
                        bi = Y.dma_start(out=x0s[k][:], in_=x0_d.ap()[128 * k:128 * (k + 1), :])
                        dx0s[k].inc(bi, 16)

                    sD.wait(V)
                    pj_hist = {}
                    for w in range(4):
                        for k in range(4):
                            ti = 4 * w + k
                            if ti + 2 < NT:
                                if ti >= 1:
                                    Y.wait_ge(vLN.h, ti)
                                bi = Y.dma_start(out=x0s[(ti + 2) % 3][:],
                                                 in_=x0_d.ap()[128 * (ti + 2):128 * (ti + 3), :])
                                dx0s[(ti + 2) % 3].inc(bi, 16)
                            dx0s[ti % 3].wait(V)
                            xs = x0s[ti % 3]
                            V.bn_stats(st1[:, 0:6], xs[:, 0:512]); V.drain()
                            V.bn_stats(st1[:, 6:12], xs[:, 512:1024]); V.drain()
                            V.bn_aggr(ag1[:], st1[:]); V.drain()
                            V.tensor_scalar(out=ve1[:], in0=ag1[:, 1:2], scalar1=EPS1, scalar2=None, op0=ALU.add)
                            y1d.inc(V.drain())
                            y1d.wait(G)
                            G.tensor_tensor(out=vi1[:], in0=ve1[:], in1=kmhalf[:], op=ALU.pow)
                            g1d.inc(G.drain())
                            g1d.wait(V)
                            if ti >= 2:
                                V.wait_ge(xprdy.h, ti - 1)
                                V.wait_ge(dh1w[ti % 2].h, 16 * (ti // 2))
                            V.tensor_scalar(out=hb1[k % 2][:], in0=xs[:], scalar1=ag1[:, 0:1],
                                            scalar2=vi1[:], op0=ALU.subtract, op1=ALU.mult)
                            V.drain()
                            vLN.inc(V.drain())
                            V.tensor_tensor(out=hb1[k % 2][:], in0=hb1[k % 2][:], in1=ca1[:], op=ALU.mult); V.drain()
                            V.tensor_tensor(out=hb1[k % 2][:], in0=hb1[k % 2][:], in1=sav[:], op=ALU.add)
                            vh1.inc(V.drain())
                            vh1.wait(Y)
                            bi = Y.dma_start(out=h1tok_d.ap()[128 * ti:128 * (ti + 1), :], in_=hb1[k % 2][:])
                            dh1w[ti % 2].inc(bi, 16)
                            vh1.wait(T)
                            xp = xpA if k % 2 == 0 else xpB
                            if ti >= 2:
                                T.wait_ge(h1wld.h, ti - 1)
                            for c in range(NCH):
                                T.transpose(xp[:, 128 * c:128 * (c + 1)], hb1[k % 2][:, 128 * c:128 * (c + 1)], ident[:])
                            xprdy.inc(T.drain())
                            xprdy.wait(C)
                            C.copy(out=h1wW[:].rearrange("p (c n) -> p c n", c=NCH)[:, :, 128 * k:128 * (k + 1)],
                                   in_=xp[:].rearrange("p (c n) -> p c n", c=NCH))
                            h1wld.inc(C.drain())

                        h1wld.wait(T)
                        for kind, wsrc, dst in (("K", wk, kt), ("Q", wq, qtF)):
                            for p in range(NPK):
                                pj = pj1 if p % 2 == 0 else pj2
                                prev = pj_hist.get(("KQ", p % 2))
                                if prev:
                                    T.wait_ge(pjev.h, prev)
                                for c in range(NCH):
                                    T.matmul(pj[:, 0:512],
                                             wsrc[:, 1024 * c + 128 * p:1024 * c + 128 * (p + 1)],
                                             h1wW[:, 512 * c:512 * (c + 1)],
                                             start=(c == 0), stop=(c == NCH - 1),
                                             skip_group_check=True)
                                pjdn.inc(T.drain())
                                pjdn.wait(C)
                                C.copy(out=dst[:, 2048 * p + 512 * w:2048 * p + 512 * (w + 1)], in_=pj[:, 0:512])
                                pjev.inc(C.drain())
                                pj_hist[("KQ", p % 2)] = pjev.n
                        for k in range(4):
                            ti = 4 * w + k
                            prev = pj_hist.get("V")
                            if prev:
                                T.wait_ge(pjev.h, prev)
                            for c in range(NCH):
                                for half in range(2):
                                    T.matmul(pjV[:, 512 * half:512 * (half + 1)],
                                             h1wW[:, 512 * c + 128 * k:512 * c + 128 * (k + 1)],
                                             wv[:, 1024 * c + 512 * half:1024 * c + 512 * (half + 1)],
                                             start=(c == 0), stop=(c == NCH - 1),
                                             skip_group_check=True)
                            pjdn.inc(T.drain())
                            pjdn.wait(C)
                            C.copy(out=vex[:].rearrange("p (s c) -> p s c", c=65)[:, H * ti:H * (ti + 1), 0:64],
                                   in_=pjV[:].rearrange("p (s c) -> p s c", c=64))
                            pjev.inc(C.drain())
                            pj_hist["V"] = pjev.n
                    p1dn.inc(T.drain())

                with contextlib.ExitStack() as p2_es:
                    sb2 = lambda n, sh, d=BF16: p2_es.enter_context(nc.sbuf_tensor(n, sh, d))
                    NATR = 16
                    atr = [sb2(f"atr{i}", [128, 1024]) for i in range(NATR)]
                    hT = [sb2(f"hT{i}", [128, D]) for i in range(4)]
                    osb = [sb2(f"osb{i}", [128, 64]) for i in range(8)]
                    linv = [sb2(f"linvsb{i}", [128, 4], F32) for i in range(2)]
                    yt = sb2("ytsb", [128, D], F32)
                    z2 = sb2("z2sb", [128, D])
                    ab = sb2("absb", [128, D])
                    ab2 = [sb2(f"ab2{i}", [128, D]) for i in range(2)]
                    h1r = [sb2(f"h1r{i}", [128, D]) for i in range(2)]
                    gav = sb2("gavsb", [128, D]); gmt = sb2("gmtsb", [128, D]); btt = sb2("bttsb", [128, D])
                    qz = [sb2(f"qz{i}", [128, 512]) for i in range(2)]

                    sC = [p2_es.enter_context(nc.psum_tensor(f"sC{i}", [128, 1024], F32)) for i in range(2)]
                    oPd = [p2_es.enter_context(nc.psum_tensor(f"oPd{i}", [128, 512], F32)) for i in range(2)]
                    xq = p2_es.enter_context(nc.psum_tensor("xq", [64, 512], BF16))
                    woP = p2_es.enter_context(nc.psum_tensor("woP", [128, 512], F32))

                    bi = G.dma_start(out=mp0a[:], in_=gam_d.ap()[:, :]); dgw.inc(bi, 16)
                    bi = G.dma_start(out=mp0b[:], in_=bet_d.ap()[:, :]); dgw.inc(bi, 16)
                    dgw.wait(T)
                    if True:
                        bp2 = woP
                        def bc2(dstt, srcrow):
                            for hh in range(2):
                                sD.wait(T)
                                T.matmul(bp2[:, 0:512], onesr[0:1, 0:128],
                                         srcrow[0:1, 512 * hh:512 * (hh + 1)], start=True, stop=True)
                                pC.inc(T.drain())
                                pC.wait(C)
                                C.copy(out=dstt[:, 512 * hh:512 * (hh + 1)], in_=bp2[:, 0:512])
                                sD.inc(C.drain())

                        bc2(gmt, mp0a)
                        bc2(btt, mp0b)
                        C.wait_ge(pC.h, pC.n)
                        bi = C.dma_start(out=mp0a[:], in_=mod[8:9, 2 * D:3 * D]); dscq.inc(bi, 16)
                        dscq.wait(T)
                        bc2(gav, mp0a)
                        sD.wait(V)

                    C.wait_ge(dh1w[0].h, 16 * NT // 2)
                    C.wait_ge(dh1w[1].h, 16 * NT // 2)
                    sc_hist = {}
                    atr_hist = {}
                    nrm_hist = {}
                    hT_hist = {}
                    qz_hist = {}
                    oxp_hist = {}
                    yadd = sm("yadd")
                    qzd = sm("qzd")
                    p1dn.wait(V)
                    qzm = sm("qzm")
                    V.memset(qz[0][:], 0.0); V.memset(qz[1][:], 0.0)
                    qzm.inc(V.drain())
                    qzm.wait(Y)
                    qzdd = [sm("qzdd0"), sm("qzdd1")]

                    def issue_qz(ghq):
                        g2, hq2 = divmod(ghq, 16)
                        e0q2 = 64 * (hq2 % 2)
                        pkq2 = hq2 // 2
                        prevq2 = qz_hist.get(hq2 % 2)
                        if prevq2:
                            Y.wait_ge(srdy.h, prevq2)
                        bi2 = Y.dma_start(
                            out=qz[hq2 % 2][e0q2:e0q2 + 64, :],
                            in_=qtF[e0q2:e0q2 + 64, 2048 * pkq2 + 512 * g2:2048 * pkq2 + 512 * (g2 + 1)])
                        qzdd[hq2 % 2].inc(bi2, 16)

                    issue_qz(0)
                    cbuf = 0
                    acnt = 0
                    for g in range(4):
                        qis = tuple(4 * g + j for j in range(4))
                        ntile = 4 * g + 4
                        nchunks = (ntile + 1) // 2
                        used_this_g = set()
                        for j in range(2):
                            qi = qis[j]
                            if g == 0 and j == 0:
                                p1dn.wait(Y)
                            if g >= 1:
                                C.wait_ge(vD.h, 4 * g)
                            bi = C.dma_start(out=h1r[j][:], in_=h1tok_d.ap()[128 * qi:128 * (qi + 1), :])
                            dh1r[j].inc(bi, 16)

                        chunk_rec = {}
                        av_first = {}
                        for hh in range(H + 2):
                            if hh < H:
                                recs = []
                                av_first[hh] = True
                                if 16 * g + hh + 1 < 64:
                                    issue_qz(16 * g + hh + 1)
                            for ci in range(nchunks):
                                if hh < H:
                                    h = hh
                                    pk = h // 2
                                    e0 = 64 * (h % 2)
                                    t_lo = 2 * ci
                                    t_hi = min(ntile, 2 * (ci + 1))
                                    buf = cbuf % 2
                                    aslot = acnt % NATR
                                    cbuf += 1; acnt += 1
                                    prev = sc_hist.get(buf)
                                    if prev:
                                        T.wait_ge(expd.h, prev)
                                    prev_av = atr_hist.get(aslot)
                                    if prev_av:
                                        C.wait_ge(avd.h, prev_av)
                                    if ci == 0:
                                        qzdd[h % 2].wait(T)
                                    for t2 in range(t_lo, t_hi):
                                        jj = t2 - t_lo
                                        T.matmul(sC[buf][:, 512 * jj:512 * (jj + 1)],
                                                 kt[:, 2048 * pk + 128 * t2:2048 * pk + 128 * (t2 + 1)],
                                                 qz[h % 2][:, :],
                                                 start=True, stop=True,
                                                 skip_group_check=True)
                                    srdy.inc(T.drain())
                                    srdy.wait(C)
                                    C.activation(out=atr[aslot][:, 0:512 * (t_hi - t_lo)],
                                                 in_=sC[buf][:, 0:512 * (t_hi - t_lo)],
                                                 func=AF.Exp, scale=0.125, bias=km20[:, 0:1])
                                    expd.inc(C.drain())
                                    sc_hist[buf] = expd.n
                                    diag = [t2 for t2 in range(t_lo, t_hi) if t2 in qis]
                                    mskn = None
                                    if diag:
                                        expd.wait(V)
                                        for t2 in diag:
                                            jj = t2 - t_lo
                                            k = t2 - 4 * g
                                            V.tensor_tensor(
                                                out=atr[aslot][:, 512 * jj + 128 * k:512 * jj + 128 * (k + 1)],
                                                in0=atr[aslot][:, 512 * jj + 128 * k:512 * jj + 128 * (k + 1)],
                                                in1=mask[:], op=ALU.mult)
                                        mskd.inc(V.drain())
                                        mskn = mskd.n
                                    recs.append((aslot, t_lo, t_hi, expd.n, diag, mskn))
                                if 1 <= hh <= H:
                                    h = hh - 1
                                    gh = 16 * g + h
                                    (aslot, t_lo, t_hi, expn, diag, mskn) = chunk_rec[h][ci]
                                    if av_first[h]:
                                        prevn = nrm_hist.get(gh - 2)
                                        if prevn:
                                            T.wait_ge(nrm.h, prevn)
                                    T.wait_ge(expd.h, expn)
                                    for j2, qi in enumerate(qis):
                                        for t2 in range(t_lo, min(t_hi, qi + 1)):
                                            if t2 == qi:
                                                continue
                                            jj = t2 - t_lo
                                            T.matmul(oPd[h % 2][:, 128 * j2:128 * j2 + 65],
                                                     atr[aslot][:, 512 * jj + 128 * j2:512 * jj + 128 * (j2 + 1)],
                                                     vex[:, 1040 * t2 + 65 * h:1040 * t2 + 65 * h + 65],
                                                     start=av_first[h], stop=False, skip_group_check=True)
                                            av_first[h] = False
                                    if diag:
                                        T.wait_ge(mskd.h, mskn)
                                        for t2 in diag:
                                            j2 = t2 - 4 * g
                                            jj = t2 - t_lo
                                            T.matmul(oPd[h % 2][:, 128 * j2:128 * j2 + 65],
                                                     atr[aslot][:, 512 * jj + 128 * j2:512 * jj + 128 * (j2 + 1)],
                                                     vex[:, 1040 * t2 + 65 * h:1040 * t2 + 65 * h + 65],
                                                     start=av_first[h], stop=False, skip_group_check=True)
                                            av_first[h] = False


                            if hh < H:
                                chunk_rec[hh] = recs
                                qz_hist[hh % 2] = srdy.n
                            if 1 <= hh <= H:
                                h = hh - 1
                                gh = 16 * g + h
                                avd.inc(T.drain())
                                for (aslot, _, _, _, _, _) in chunk_rec[h]:
                                    atr_hist[aslot] = avd.n
                                avd.wait(V)
                                os_ = osb[4 * (h % 2):4 * (h % 2) + 4]
                                if gh >= 2:
                                    V.wait_ge(oxp.h, oxp_hist[gh - 2])
                                for j2 in range(4):
                                    V.reciprocal(linv[h % 2][:, j2:j2 + 1], oPd[h % 2][:, 128 * j2 + 64:128 * j2 + 65])
                                V.drain()
                                for j2 in range(4):
                                    V.tensor_scalar(out=os_[j2][:], in0=oPd[h % 2][:, 128 * j2:128 * j2 + 64],
                                                    scalar1=linv[h % 2][:, j2:j2 + 1], scalar2=None, op0=ALU.mult)
                                nrm.inc(V.drain())
                                nrm_hist[gh] = nrm.n
                            if 2 <= hh:
                                h = hh - 2
                                gh = 16 * g + h
                                pk = h // 2
                                e0 = 64 * (h % 2)
                                os_ = osb[4 * (h % 2):4 * (h % 2) + 4]
                                T.wait_ge(nrm.h, nrm_hist[gh])
                                if gh >= 1:
                                    T.wait_ge(hTd.h, hT_hist[gh - 1])
                                for j2 in range(4):
                                    T.transpose(xq[0:64, 128 * j2:128 * (j2 + 1)],
                                                os_[j2][:], ident[:])
                                oxp.inc(T.drain())
                                oxp_hist[gh] = oxp.n
                                oxp.wait(V)
                                if h == 0 and g >= 1:
                                    V.wait_ge(wod.h, 8 * g)
                                for j2 in range(4):
                                    V.tensor_copy(out=hT[j2][e0:e0 + 64, 128 * pk:128 * (pk + 1)],
                                                  in_=xq[0:64, 128 * j2:128 * (j2 + 1)])
                                hTd.inc(V.drain())
                                hT_hist[gh] = hTd.n

                        hTd.wait(T)
                        for j, qi in enumerate(qis):
                            for half in range(2):
                                if wod.n >= 1:
                                    T.wait_ge(yadd.h, wod.n)
                                for c in range(NCH):
                                    T.matmul(woP[:, 0:512],
                                             hT[j][:, 128 * c:128 * (c + 1)],
                                             wosb[:, 1024 * c + 512 * half:1024 * c + 512 * (half + 1)],
                                             start=(c == 0), stop=(c == NCH - 1),
                                             skip_group_check=True)
                                wod.inc(T.drain())
                                wod.wait(V)
                                if half == 0:
                                    V.wait_ge(dh1r[j % 2].h, dh1r[j % 2].n)
                                V.tensor_tensor(out=yt[:, 512 * half:512 * (half + 1)], in0=woP[:, 0:512],
                                                in1=h1r[j % 2][:, 512 * half:512 * (half + 1)], op=ALU.add)
                                yadd.inc(V.drain())
                            vD.inc(V.drain())
                            V.bn_stats(st2[:, 0:6], yt[:, 0:512]); V.drain()
                            V.bn_stats(st2[:, 6:12], yt[:, 512:1024]); V.drain()
                            V.bn_aggr(ag2[:], st2[:]); V.drain()
                            V.tensor_scalar(out=ve2[:], in0=ag2[:, 1:2], scalar1=EPS2, scalar2=None, op0=ALU.add)
                            yLd.inc(V.drain())
                            yLd.wait(G)
                            G.tensor_tensor(out=vi2[:], in0=ve2[:], in1=kmhalf[:], op=ALU.pow)
                            gLd.inc(G.drain())
                            gLd.wait(V)
                            V.tensor_scalar(out=z2[:], in0=yt[:], scalar1=ag2[:, 0:1], scalar2=vi2[:],
                                            op0=ALU.subtract, op1=ALU.mult)
                            V.drain()
                            V.tensor_tensor(out=z2[:], in0=z2[:], in1=gmt[:], op=ALU.mult); V.drain()
                            V.tensor_tensor(out=ab[:], in0=z2[:], in1=btt[:], op=ALU.add); V.drain()
                            par = qi % 2
                            if qi >= 2:
                                V.wait_ge(dx1s[par].h, 16 * (qi // 2))
                            V.tensor_tensor(out=ab2[par][:], in0=ab[:], in1=gav[:], op=ALU.mult)
                            x1dn.inc(V.drain())
                            x1dn.wait(Y)
                            bi = Y.dma_start(
                                out=x1send_h[qi // 8].ap()[qi % 8, :, :],
                                in_=ab2[par][:])
                            dx1s[par].inc(bi, 16)
                            if j + 2 < 4:
                                qn = qis[j + 2]
                                C.wait_ge(vD.h, vD.n)
                                bi = C.dma_start(out=h1r[j % 2][:], in_=h1tok_d.ap()[128 * qn:128 * (qn + 1), :])
                                dh1r[j % 2].inc(bi, 16)

                        if g == 1:
                            from concourse.aot_env import allow_ici_all_to_all
                            G.wait_ge(dx1s[0].h, 64); G.wait_ge(dx1s[1].h, 64)
                            with allow_ici_all_to_all():
                                bi = G.collective_compute(
                                    "AllToAll", ALU.bypass,
                                    replica_groups=[list(range(8))],
                                    ins=[x1send_h[0].ap()], outs=[x1recv_h[0].ap()],
                                )
                            ccs.inc(bi, 1)

                pE.inc(T.drain())
                if dbg:
                    dx1s[0].wait(G); dx1s[1].wait(G)
                    for hh in range(2):
                        bi = G.dma_start(
                            out=dbg_x1.ap()[1024 * hh:1024 * (hh + 1), :],
                            in_=x1send_h[hh].ap().rearrange("b r d -> (b r) d")[:, :])
                        dgw.inc(bi, 16)

            from concourse.aot_env import allow_ici_all_to_all
            dx1s[0].wait(G); dx1s[1].wait(G)
            with allow_ici_all_to_all():
                bi = G.collective_compute(
                    "AllToAll", ALU.bypass,
                    replica_groups=[list(range(8))],
                    ins=[x1send_h[1].ap()], outs=[x1recv_h[1].ap()],
                )
            ccs2.inc(bi, 1)

            h2rdy = sm("h2rdy"); xfd = sm("xfd"); hevc = sm("hevc")
            grd = sm("grd"); gevc = sm("gevc"); cupd = sm("cupd")
            bcd = sm("bcd"); bce = sm("bce"); zfd = sm("zfd")
            lnf = sm("lnf"); glnf = sm("glnf"); c0d = sm("c0d")

            with contextlib.ExitStack() as ffA:
                sbf = lambda n, sh, d=BF16: ffA.enter_context(nc.sbuf_tensor(n, sh, d))
                h2T = sbf("h2T", [128, B * 2048])
                cst = sbf("cst", [128, 2048])
                uoo = sbf("uoo", [128, 2048])
                gbt = [sbf(f"gbt{i}", [128, D]) for i in range(2)]

                with contextlib.ExitStack() as ffB:
                    sbb = lambda n, sh, d=BF16: ffB.enter_context(nc.sbuf_tensor(n, sh, d))
                    gwt = {}
                    for nm in ("wfT", "ufT", "wiT", "uiT", "wcT", "ucT"):
                        gwt[nm] = sbb(nm + "sb", [128, 8192])
                    bp0 = [sbb(f"bp0{i}", [1, D]) for i in range(3)]
                    cb1 = [sbb(f"cb1{i}", [128, D]) for i in range(2)]
                    sbt = [sbb(f"sbt{i}", [128, D]) for i in range(2)]
                    fsb = sbb("fsb", [128, 2048])
                    isb = sbb("isb", [128, 2048])
                    gsb = sbb("gsb", [128, 2048])
                    x1s = [sbb(f"x1s{i}", [128, D], F32) for i in range(2)]
                    abr = [sbb(f"abr{i}", [128, D]) for i in range(2)]
                    zf = sbb("zf", [128, D])
                    h2tok = [sbb(f"h2tok{i}", [128, D]) for i in range(2)]

                    pE.wait(G)
                    G.wait_ge(x1dn.h, x1dn.n)
                    G.wait_ge(hTd.h, hTd.n)
                    dgr = sm("dgr")
                    for i, r in enumerate((0, 1, 2)):
                        bi = G.dma_start(out=bp0[i][:], in_=brow_d.ap()[r:r + 1, :])
                        dgr.inc(bi, 16)
                    C.wait_ge(pC.h, pC.n)
                    bi = G.dma_start(out=mp0a[:], in_=brow_d.ap()[4:5, :])
                    dgr.inc(bi, 16)
                    dgf3 = [sm(f"dgf3{i}") for i in range(3)]
                    for gi2, pair in enumerate((("wfT", wfT_d, "ufT", ufT_d),
                                                ("wiT", wiT_d, "uiT", uiT_d),
                                                ("wcT", wcT_d, "ucT", ucT_d))):
                        for nm, srcd in (pair[0:2], pair[2:4]):
                            bi = G.dma_start(out=gwt[nm][:].rearrange("p (c n) -> p c n", c=8),
                                             in_=srcd.ap().rearrange("(c p) n -> p c n", p=128))
                            dgf3[gi2].inc(bi, 16)

                    xpf = [ffB.enter_context(nc.psum_tensor(f"xpf{i}", [128, 1024], BF16)) for i in range(2)]
                    bpf = ffB.enter_context(nc.psum_tensor("bpf", [128, 512], F32))
                    rg = ffB.enter_context(nc.psum_tensor("Rg", [128, 2048], F32))

                    def ff_bcast(dstt, srcrow, guard=None):
                        for hh in range(2):
                            bce.wait(T)
                            T.matmul(bpf[:, 0:512], onesr[0:1, 0:128],
                                     srcrow[0:1, 512 * hh:512 * (hh + 1)], start=True, stop=True)
                            bcd.inc(T.drain())
                            bcd.wait(C)
                            if guard is not None:
                                C.wait_ge(guard[0].h, guard[1])
                            C.copy(out=dstt[:, 512 * hh:512 * (hh + 1)], in_=bpf[:, 0:512])
                            bce.inc(C.drain())

                    dgr.wait(T)
                    for mi in range(NCH):
                        bce.wait(T)
                        T.matmul(bpf[:, 0:256], mp0a[0:1, 128 * mi:128 * (mi + 1)], onesr[0:1, 0:256],
                                 start=True, stop=True)
                        c0d.inc(T.drain())
                        c0d.wait(C)
                        C.copy(out=cst[:, 256 * mi:256 * (mi + 1)], in_=bpf[:, 0:256])
                        bce.inc(C.drain())

                    ccs.wait(Y)
                    c0d.wait(C)
                    def prep_bcast(t):
                        tp2 = t % 2
                        if t >= 2:
                            C.wait_ge(h2rdy.h, 2 * t - 2)
                        C.wait_ge(bcd.h, bcd.n)
                        bi2 = C.dma_start(out=mp0a[:], in_=mod[t:t + 1, 4 * D:5 * D]); dscq.inc(bi2, 16)
                        bi2 = C.dma_start(out=mp0b[:], in_=mod[t:t + 1, 3 * D:4 * D]); dscq.inc(bi2, 16)
                        dscq.wait(T)
                        ff_bcast(cb1[tp2], mp0a)
                        ff_bcast(sbt[tp2], mp0b)
                    def prep_ln(t, rt):
                        tp = t % 2
                        it = 2 * t + rt
                        sl = it % 2
                        if it >= 2:
                            Y.wait_ge(zfd.h, it - 1)
                        if rt == 1 and t == 0:
                            ccs2.wait(Y)
                        bi = Y.dma_start(out=x1s[sl][:], in_=x0f_d.ap()[t, 128 * rt:128 * (rt + 1), :])
                        dxr[sl].inc(bi, 16)
                        bi = Y.dma_start(out=abr[sl][:], in_=x1recv_h[rt].ap()[t, :, :])
                        dxr[sl].inc(bi, 16)
                        V.wait_ge(dxr[sl].h, dxr[sl].n)
                        V.tensor_tensor(out=x1s[sl][:], in0=x1s[sl][:], in1=abr[sl][:], op=ALU.add)
                        V.drain()
                        V.bn_stats(st1[:, 0:6], x1s[sl][:, 0:512]); V.drain()
                        V.bn_stats(st1[:, 6:12], x1s[sl][:, 512:1024]); V.drain()
                        V.bn_aggr(ag1[:], st1[:]); V.drain()
                        V.tensor_scalar(out=ve1[:], in0=ag1[:, 1:2], scalar1=EPS1, scalar2=None, op0=ALU.add)
                        lnf.inc(V.drain())
                        lnf.wait(G)
                        G.tensor_tensor(out=vi1[:], in0=ve1[:], in1=kmhalf[:], op=ALU.pow)
                        glnf.inc(G.drain())
                        glnf.wait(V)
                        V.tensor_scalar(out=zf[:], in0=x1s[sl][:], scalar1=ag1[:, 0:1], scalar2=vi1[:],
                                        op0=ALU.subtract, op1=ALU.mult)
                        zfd.inc(V.drain())
                        V.wait_ge(bce.h, 8 + 4 * (t + 1))
                        if it >= 2:
                            V.wait_ge(xfd.h, it - 1)
                        V.tensor_tensor(out=h2tok[sl][:], in0=zf[:], in1=cb1[tp][:], op=ALU.mult); V.drain()
                        V.tensor_tensor(out=h2tok[sl][:], in0=h2tok[sl][:], in1=sbt[tp][:], op=ALU.add)
                        h2rdy.inc(V.drain())

                    def prep_tr(t, rt):
                        it = 2 * t + rt
                        sl = it % 2
                        T.wait_ge(h2rdy.h, it + 1)
                        if it >= 2:
                            T.wait_ge(hevc.h, it - 1)
                        for c in range(NCH):
                            T.transpose(xpf[sl][:, 128 * c:128 * (c + 1)], h2tok[sl][:, 128 * c:128 * (c + 1)], ident[:])
                        xfd.inc(T.drain())
                        xfd.wait(C)
                        C.copy(out=h2T[:].rearrange("p (t c r) -> p t c r", t=B, c=NCH)
                               [:, t, :, 128 * rt:128 * (rt + 1)],
                               in_=xpf[sl][:].rearrange("p (c r) -> p c r", c=NCH))
                        hevc.inc(C.drain())

                    gates = (("wfT", "ufT", 0, AF.Sigmoid, fsb), ("wiT", "uiT", 1, AF.Sigmoid, isb),
                             ("wcT", "ucT", 2, AF.Tanh, gsb))

                    def gate_step(t):
                        for gi, (wn, un, bidx, fn, dstt) in enumerate(gates):
                            if t == 0:
                                T.wait_ge(dgf3[gi].h, 32)
                                if gi == 0:
                                    T.wait_ge(bce.h, 8)
                            gevc.wait(T)
                            T.wait_ge(hevc.h, 2 * (t + 1))
                            T.wait_ge(cupd.h, t)
                            for bank in range(4):
                                for k in range(2):
                                    mi = 2 * bank + k
                                    T.matmul(rg[:, 256 * mi:256 * (mi + 1)],
                                             bp0[bidx][0:1, 128 * mi:128 * (mi + 1)], onesr[0:1, 0:256],
                                             start=(k == 0), stop=False, skip_group_check=True)
                                for k in range(2):
                                    mi = 2 * bank + k
                                    for c in range(NCH):
                                        T.matmul(rg[:, 256 * mi:256 * (mi + 1)],
                                                 gwt[wn][:, 1024 * c + 128 * mi:1024 * c + 128 * (mi + 1)],
                                                 h2T[:, 2048 * t + 256 * c:2048 * t + 256 * (c + 1)],
                                                 start=False, stop=False, skip_group_check=True)
                                for k in range(2):
                                    mi = 2 * bank + k
                                    for c in range(NCH):
                                        T.matmul(rg[:, 256 * mi:256 * (mi + 1)],
                                                 gwt[un][:, 1024 * c + 128 * mi:1024 * c + 128 * (mi + 1)],
                                                 cst[:, 256 * c:256 * (c + 1)],
                                                 start=False, stop=(k == 1 and c == NCH - 1),
                                                 skip_group_check=True)
                            grd.inc(T.drain())
                            grd.wait(C)
                            C.activation(out=dstt[:], in_=rg[:, 0:2048], func=fn)
                            gevc.inc(C.drain())
                        gevc.wait(V)
                        V.tensor_tensor(out=fsb[:], in0=fsb[:], in1=cst[:], op=ALU.mult); V.drain()
                        V.tensor_tensor(out=isb[:], in0=isb[:], in1=gsb[:], op=ALU.mult); V.drain()
                        V.tensor_tensor(out=cst[:], in0=fsb[:], in1=isb[:], op=ALU.add)
                        cupd.inc(V.drain())

                    prep_bcast(0); prep_bcast(1)
                    prep_ln(0, 0); prep_ln(0, 1)
                    prep_tr(0, 0); prep_tr(0, 1)
                    prep_bcast(2)
                    prep_ln(1, 0); prep_ln(1, 1)
                    for t in range(B):
                        gate_step(t)
                        if t + 1 < B:
                            prep_tr(t + 1, 0); prep_tr(t + 1, 1)
                        if t + 3 < B:
                            prep_bcast(t + 3)
                        if t + 2 < B:
                            prep_ln(t + 2, 0); prep_ln(t + 2, 1)
                    if dbg:
                        G.wait_ge(hevc.h, hevc.n)
                        bi = G.dma_start(out=dbg_h2.ap()[:, :], in_=h2T[:, :])
                        dgf.inc(bi, 16)
                    if dbg:
                        G.wait_ge(cupd.h, cupd.n)
                        bi = G.dma_start(out=dbg_c.ap()[:, :], in_=cst[:, :])
                        dgf.inc(bi, 16)

                with contextlib.ExitStack() as ffC:
                    sbc = lambda n, sh, d=BF16: ffC.enter_context(nc.sbuf_tensor(n, sh, d))
                    woTs = sbc("woTs", [128, 8192])
                    uoTs = sbc("uoTs", [128, 8192])
                    bp0o = sbc("bp0o", [1, D])
                    gbt8 = [sbc(f"gbt8{i}", [128, D]) for i in range(B)]
                    x1f = [sbc(f"x1f{i}", [128, D], F32) for i in range(2)]
                    abf = [sbc(f"abf{i}", [128, D]) for i in range(2)]
                    tmpf = [sbc(f"tmpf{i}", [128, D], F32) for i in range(2)]
                    hb = [sbc(f"hb{i}", [128, D]) for i in range(2)]
                    outt = [sbc(f"outt{i}", [128, D], F32) for i in range(2)]
                    wps = [ffC.enter_context(nc.psum_tensor(f"wps{i}", [128, 1024], F32)) for i in range(2)]
                    bpo = ffC.enter_context(nc.psum_tensor("bpo", [128, 512], F32))

                    uod = sm("uod"); uoe = sm("uoe"); wpd = sm("wpd")
                    fvd = sm("fvd"); ftd = sm("ftd"); odn = sm("odn")
                    fbd = sm("fbd"); fbe = sm("fbe")

                    G.wait_ge(cupd.h, cupd.n)
                    G.wait_ge(gevc.h, gevc.n)
                    dgu = sm("dgu")
                    bi = G.dma_start(out=bp0o[:], in_=brow_d.ap()[3:4, :])
                    dgu.inc(bi, 16)
                    bi = G.dma_start(out=uoTs[:].rearrange("p (c n) -> p c n", c=8),
                                     in_=uoT_d.ap().rearrange("(c p) n -> p c n", p=128))
                    dgu.inc(bi, 16)
                    bi = G.dma_start(out=woTs[:].rearrange("p (c n) -> p c n", c=8),
                                     in_=woT_d.ap().rearrange("(c p) n -> p c n", p=128))
                    dgf.inc(bi, 16)

                    def fin_bcast(dstt, srcrow):
                        for hh in range(2):
                            fbe.wait(T)
                            T.matmul(bpo[:, 0:512], onesr[0:1, 0:128],
                                     srcrow[0:1, 512 * hh:512 * (hh + 1)], start=True, stop=True)
                            fbd.inc(T.drain())
                            fbd.wait(C)
                            C.copy(out=dstt[:, 512 * hh:512 * (hh + 1)], in_=bpo[:, 0:512])
                            fbe.inc(C.drain())

                    dgu.wait(T)
                    cupd.wait(T)
                    cupd.wait(Y)
                    for rt in range(2):
                        for half in range(2):
                            T.matmul(wps[rt][:, 512 * half:512 * (half + 1)],
                                     onesr[0:1, 0:128], bp0o[0:1, 512 * half:512 * (half + 1)],
                                     start=True, stop=False, skip_group_check=True)
                            for c in range(NCH):
                                T.matmul(wps[rt][:, 512 * half:512 * (half + 1)],
                                         cst[:, 256 * c + 128 * rt:256 * c + 128 * (rt + 1)],
                                         uoTs[:, 1024 * c + 512 * half:1024 * c + 512 * (half + 1)],
                                         start=False, stop=(c == NCH - 1), skip_group_check=True)
                        uod.inc(T.drain())
                        uod.wait(C)
                        C.copy(out=uoo[:, 1024 * rt:1024 * (rt + 1)], in_=wps[rt][:, 0:1024])
                        uoe.inc(C.drain())

                    def gb_build(t):
                        C.wait_ge(fbd.h, fbd.n)
                        bi = C.dma_start(out=mp0b[:], in_=mod[t:t + 1, 5 * D:6 * D]); dscq.inc(bi, 16)
                        dscq.wait(T)
                        fin_bcast(gbt8[t], mp0b)

                    gb_build(0); gb_build(1)
                    for t in range(B):
                        for rt in range(2):
                            it = 2 * t + rt
                            sl = it % 2
                            T.wait_ge(uoe.h, 2)
                            if it == 0:
                                dgf.wait(T)
                            if it >= 2:
                                T.wait_ge(fvd.h, it - 1)
                            for half in range(2):
                                for c in range(NCH):
                                    T.matmul(wps[sl][:, 512 * half:512 * (half + 1)],
                                             h2T[:, 2048 * t + 256 * c + 128 * rt:2048 * t + 256 * c + 128 * (rt + 1)],
                                             woTs[:, 1024 * c + 512 * half:1024 * c + 512 * (half + 1)],
                                             start=(c == 0), stop=(c == NCH - 1), skip_group_check=True)
                            wpd.inc(T.drain())
                            if rt == 1 and t + 2 < B:
                                gb_build(t + 2)
                            if it >= 2:
                                Y.wait_ge(odn.h, it - 1)
                            bi = Y.dma_start(out=x1f[sl][:], in_=x0f_d.ap()[t, 128 * rt:128 * (rt + 1), :])
                            dxf.inc(bi, 16)
                            bi = Y.dma_start(out=abf[sl][:], in_=x1recv_h[rt].ap()[t, :, :])
                            dxf.inc(bi, 16)
                            wpd.wait(V)
                            V.wait_ge(uoe.h, 2)
                            if it >= 2:
                                V.wait_ge(ftd.h, it - 1)
                            V.tensor_tensor(out=tmpf[sl][:], in0=wps[sl][:, 0:1024],
                                            in1=uoo[:, 1024 * rt:1024 * (rt + 1)], op=ALU.add)
                            fvd.inc(V.drain())
                            fvd.wait(C)
                            C.activation(out=hb[sl][:], in_=tmpf[sl][:], func=AF.Tanh)
                            ftd.inc(C.drain())
                            ftd.wait(V)
                            V.wait_ge(fbe.h, 2 * (t + 1))
                            V.tensor_tensor(out=hb[sl][:], in0=hb[sl][:], in1=gbt8[t][:], op=ALU.mult); V.drain()
                            V.wait_ge(dxf.h, 32 * (it + 1))
                            V.tensor_tensor(out=x1f[sl][:], in0=x1f[sl][:], in1=abf[sl][:], op=ALU.add)
                            V.drain()
                            if it >= 2:
                                V.wait_ge(dout[sl].h, 16 * (it // 2))
                            V.tensor_tensor(out=outt[sl][:], in0=x1f[sl][:], in1=hb[sl][:], op=ALU.add)
                            odn.inc(V.drain())
                            odn.wait(Y)
                            bi = Y.dma_start(out=out_d.ap()[t, 128 * rt:128 * (rt + 1), :], in_=outt[sl][:])
                            dout[sl].inc(bi, 16)

                    Y.wait_ge(dout[0].h, dout[0].n)
                    Y.wait_ge(dout[1].h, dout[1].n)
                    if dbg:
                        G.wait_ge(dgf.h, dgf.n)
    return nc


_NC_CACHE = {}


def _get_nc(dbg=False):
    key = bool(dbg)
    if key not in _NC_CACHE:
        _NC_CACHE[key] = build_nc(dbg=key)
    return _NC_CACHE[key]


def make_in_maps(inputs):
    x = np.ascontiguousarray(np.asarray(inputs["L0_attn_x"]), dtype=np.float32)
    c = np.ascontiguousarray(np.asarray(inputs["L0_c"]), dtype=np.float32)
    wmod = np.ascontiguousarray(np.asarray(inputs["L0_wmod"]), dtype=np.float32)
    wq = np.ascontiguousarray(np.asarray(inputs["L0_attn_Wq"]).transpose(1, 0, 2).reshape(D, D), dtype=np.float32)
    wk = np.ascontiguousarray(np.asarray(inputs["L0_attn_Wk"]).transpose(1, 0, 2).reshape(D, D), dtype=np.float32)
    wv = np.ascontiguousarray(np.asarray(inputs["L0_attn_Wv"]).transpose(1, 0, 2).reshape(D, D), dtype=np.float32)
    wo = np.ascontiguousarray(np.asarray(inputs["L0_attn_Wo"]), dtype=np.float32)
    gam = np.ascontiguousarray(np.asarray(inputs["L0_attn_gamma"]), dtype=np.float32).reshape(1, D)
    bet = np.ascontiguousarray(np.asarray(inputs["L0_attn_beta"]), dtype=np.float32).reshape(1, D)
    trs = lambda k: np.ascontiguousarray(np.asarray(inputs[k], dtype=np.float32).T)
    wfT, ufT = trs("L0_ffn_wf"), trs("L0_ffn_uf")
    wiT, uiT = trs("L0_ffn_wi"), trs("L0_ffn_ui")
    wcT, ucT = trs("L0_ffn_wc"), trs("L0_ffn_uc")
    woT, uoT = trs("L0_ffn_wo"), trs("L0_ffn_uo")
    brows = np.ascontiguousarray(np.stack([
        np.asarray(inputs["L0_ffn_bf"], dtype=np.float32),
        np.asarray(inputs["L0_ffn_bi"], dtype=np.float32),
        np.asarray(inputs["L0_ffn_bc"], dtype=np.float32),
        np.asarray(inputs["L0_ffn_bo"], dtype=np.float32),
        np.asarray(inputs["L0_ffn_in_c"], dtype=np.float32),
    ]))
    ident = np.eye(128, dtype=BF)
    masktile = np.triu(np.ones((128, 128), dtype=np.float32)).astype(BF)
    onesrow = np.ones((1, 512), dtype=BF)
    shared = dict(wmod=wmod, wq=wq, wk=wk, wv=wv, wo=wo, gamma=gam, beta=bet,
                  wfT=wfT, ufT=ufT, wiT=wiT, uiT=uiT, wcT=wcT, ucT=ucT,
                  woT=woT, uoT=uoT, brows=brows, ident=ident,
                  masktile=masktile, onesrow=onesrow)
    in_maps = []
    for b in range(B):
        m = dict(shared)
        m["x0"] = np.ascontiguousarray(x[b])
        rows = np.r_[128 * b:128 * (b + 1), 1024 + 128 * b:1024 + 128 * (b + 1)]
        m["x0f"] = np.ascontiguousarray(x[:, rows, :])
        m["c9"] = np.ascontiguousarray(np.concatenate([c, c[b:b + 1]], axis=0))
        in_maps.append(m)
    return in_maps


def kernel(**inputs):
    nc = _get_nc(dbg=False)
    in_maps = make_in_maps(inputs)
    res = run_bass_kernel_spmd(nc, in_maps, core_ids=list(range(8)))
    full = np.empty((B, S, D), dtype=np.float32)
    for j in range(8):
        o = np.asarray(res.results[j]["out"])
        full[:, 128 * j:128 * (j + 1), :] = o[:, 0:128, :]
        full[:, 1024 + 128 * j:1024 + 128 * (j + 1), :] = o[:, 128:256, :]
    return full
